# version 04
# baseline (speedup 1.0000x reference)
import math
import numpy as np
import ml_dtypes
from contextlib import ExitStack

import concourse.bass as bass
import concourse.mybir as mybir
from concourse.bass_utils import run_bass_kernel_spmd

F32 = mybir.dt.float32
BF16 = mybir.dt.bfloat16
I32 = mybir.dt.int32
AF = mybir.ActivationFunctionType
ALU = mybir.AluOpType
AX = mybir.AxisListType

NT = 1024
D = 4096
DPT = 32
H = 11008
HTN = 86
EPS = 1e-5
RS128 = 1.0 / math.sqrt(128.0)
WCH = 16

ENGS = ("sync", "gpsimd", "vector", "scalar", "tensor")


class KB:

    def __init__(self):
        self.ops = {e: [] for e in ENGS}
        self.cnt = {s: 0 for s in ("SP", "SV", "SS", "LG", "LY", "WB", "SO")}

    def bump(self, s, k=1):
        self.cnt[s] += k
        return self.cnt[s]


def build_nc(debug_dump=(), sub_layers=("ffn1", "attn", "ffn2")):
    nc = bass.Bass("TRN2", target_bir_lowering=False)

    x_d = nc.dram_tensor("x", [NT, D], F32, kind="ExternalInput")
    wq_d = nc.dram_tensor("wq", [D, 128, 32], F32, kind="ExternalInput")
    bq_d = nc.dram_tensor("bq", [128, 32], F32, kind="ExternalInput")
    wk_d = nc.dram_tensor("wk", [D, 128, 8], F32, kind="ExternalInput")
    bk_d = nc.dram_tensor("bk", [128, 8], F32, kind="ExternalInput")
    wv_d = nc.dram_tensor("wv", [D, 128, 8], F32, kind="ExternalInput")
    bv_d = nc.dram_tensor("bv", [128, 8], F32, kind="ExternalInput")
    wg_d = nc.dram_tensor("wg", [H, D], I32, kind="ExternalInput")
    wu_d = nc.dram_tensor("wu", [H, D], I32, kind="ExternalInput")
    wd_d = nc.dram_tensor("wd", [D, H], I32, kind="ExternalInput")
    sg_d = nc.dram_tensor("sg", [1, 1], F32, kind="ExternalInput")
    su_d = nc.dram_tensor("su", [1, 1], F32, kind="ExternalInput")
    sd_d = nc.dram_tensor("sd", [1, 1], F32, kind="ExternalInput")
    id_d = nc.dram_tensor("idm", [128, 128], BF16, kind="ExternalInput")

    out_d = nc.dram_tensor("out", [NT, D], F32, kind="ExternalOutput")
    dbg = {}
    dbg2_d = None
    for name in debug_dump:
        if name == "small":
            dbg2_d = nc.dram_tensor("dbg_small", [128, 8192], F32, kind="ExternalOutput")
            continue
        dbg[name] = nc.dram_tensor("dbg_" + name, [128, DPT * NT], F32,
                                   kind="ExternalOutput")

    wgT_s = nc.dram_tensor("wgT_s", [HTN, 128, 4096], BF16)
    wuT_s = nc.dram_tensor("wuT_s", [HTN, 128, 4096], BF16)
    wdT_s = nc.dram_tensor("wdT_s", [DPT, HTN, 128, 128], BF16)
    mid_s = nc.dram_tensor("mid_s", [HTN, 128, NT], BF16)

    kb = KB()

    with ExitStack() as stk:
        E = stk.enter_context
        Xt = E(nc.sbuf_tensor("Xt", [128, DPT * NT], BF16))
        ident = E(nc.sbuf_tensor("ident", [128, 128], BF16))
        onesb = E(nc.sbuf_tensor("onesb", [128, 1], BF16))
        ones1f = E(nc.sbuf_tensor("ones1f", [1, 128], F32))
        epst = E(nc.sbuf_tensor("epst", [1, 1], F32))
        mub = E(nc.sbuf_tensor("mub", [128, NT], F32))
        nrstdb = E(nc.sbuf_tensor("nrstdb", [128, NT], F32))
        s1row = E(nc.sbuf_tensor("s1row", [1, NT], F32))
        s2row = E(nc.sbuf_tensor("s2row", [1, NT], F32))
        murow = E(nc.sbuf_tensor("murow", [1, NT], F32))
        varow = E(nc.sbuf_tensor("varow", [1, NT], F32))
        nrsrow = E(nc.sbuf_tensor("nrsrow", [1, NT], F32))
        sqt = [E(nc.sbuf_tensor(f"sqt{i}", [128, NT], BF16)) for i in range(2)]
        rs_g = E(nc.sbuf_tensor("rs_g", [128, HTN], F32))
        rs_u = E(nc.sbuf_tensor("rs_u", [128, HTN], F32))
        sg128 = E(nc.sbuf_tensor("sg128", [128, 1], F32))
        su128 = E(nc.sbuf_tensor("su128", [128, 1], F32))
        sd128 = E(nc.sbuf_tensor("sd128", [128, 1], F32))
        cu128 = E(nc.sbuf_tensor("cu128", [128, 1], F32))
        btq = E(nc.sbuf_tensor("btq", [128, 8], F32))
        bkv = E(nc.sbuf_tensor("bkv", [128, 16], F32))
        bqraw = E(nc.sbuf_tensor("bqraw", [128, 32], F32))

        pb = [E(nc.psum_tensor(f"pb{i}", [128, 512], F32)) for i in range(8)]
        sem = {s: E(nc.semaphore(s)) for s in ("SP", "SV", "SS", "LG", "LY", "WB", "SO")}
        block = E(nc.Block())

        def ev(eng, fn):
            kb.ops[eng].append(fn)

        def wait(eng, s, v):
            if v > 0:
                kb.ops[eng].append(lambda e, s=s, v=v: e.wait_ge(sem[s], v))

        ev("sync", lambda e: e.dma_start(out=ident[:, :], in_=id_d.ap()).then_inc(sem["LY"], 16))
        ev("sync", lambda e: e.dma_start(out=bqraw[:, :], in_=bq_d.ap()).then_inc(sem["LY"], 16))
        ev("sync", lambda e: e.dma_start(out=bkv[:, 0:8], in_=bk_d.ap()).then_inc(sem["LY"], 16))
        ev("sync", lambda e: e.dma_start(out=bkv[:, 8:16], in_=bv_d.ap()).then_inc(sem["LY"], 16))
        ev("sync", lambda e: e.dma_start(out=sg128[:, :], in_=sg_d.ap().broadcast_to([128, 1])).then_inc(sem["LY"], 16))
        ev("sync", lambda e: e.dma_start(out=su128[:, :], in_=su_d.ap().broadcast_to([128, 1])).then_inc(sem["LY"], 16))
        ev("sync", lambda e: e.dma_start(out=sd128[:, :], in_=sd_d.ap().broadcast_to([128, 1])).then_inc(sem["LY"], 16))
        ly_consts = kb.bump("LY", 7 * 16)

        wait("vector", "LY", ly_consts)
        ev("vector", lambda e: e.memset(onesb[:, :], 1.0))
        ev("vector", lambda e: e.memset(ones1f[:, :], 1.0))
        ev("vector", lambda e: e.memset(epst[:, :], EPS))
        ev("vector", lambda e: e.tensor_mul(cu128[:, :], su128[:, :], sd128[:, :]).then_inc(sem["SV"], 1))
        v1 = kb.bump("SV")
        wait("vector", "SV", v1)
        ev("vector", lambda e: e.tensor_scalar_mul(cu128[:, :], cu128[:, :], 0.5).then_inc(sem["SV"], 1))
        kb.bump("SV")
        ev("vector", lambda e: e.reduce_sum(
            btq[:, :], bqraw[:, :].rearrange("p (g j) -> p g j", g=8),
            axis=AX.X).then_inc(sem["SV"], 1))
        kb.bump("SV")

        with ExitStack() as ph:
            xin_f = [ph.enter_context(nc.sbuf_tensor(f"xinf{i}", [128, 4096], F32)) for i in range(2)]
            xin_b = [ph.enter_context(nc.sbuf_tensor(f"xinb{i}", [128, 4096], BF16)) for i in range(2)]

            ld_m = [0] * 8
            cast_m = [0] * 8
            tp_m = [0] * 8
            bank_ev = {0: 0, 1: 0}
            for tp in range(8):
                b = tp % 2
                if tp >= 2:
                    wait("sync", "SS", cast_m[tp - 2])
                ev("sync", lambda e, tp=tp, b=b: e.dma_start(
                    out=xin_f[b][:, :], in_=x_d.ap()[tp * 128:(tp + 1) * 128, :]).then_inc(sem["LY"], 16))
                ld_m[tp] = kb.bump("LY", 16)
                wait("scalar", "LY", ld_m[tp])
                if tp >= 2:
                    wait("scalar", "SP", tp_m[tp - 2])
                ev("scalar", lambda e, b=b: e.activation(
                    out=xin_b[b][:, :], in_=xin_f[b][:, :], func=AF.Copy).then_inc(sem["SS"], 1))
                cast_m[tp] = kb.bump("SS")
                wait("tensor", "SS", cast_m[tp])
                for a in range(4):
                    bank = a % 2
                    wait("tensor", "SV", bank_ev[bank])
                    for jj in range(8):
                        j = a * 8 + jj
                        last = (jj == 7)
                        def tf(e, b=b, j=j, bank=bank, jj=jj, last=last):
                            ins = e.transpose(
                                pb[bank][:, :].bitcast(BF16)[:, jj * 128:(jj + 1) * 128],
                                xin_b[b][:, j::32], ident[:, :])
                            if last:
                                ins.then_inc(sem["SP"], 1)
                        ev("tensor", tf)
                    gm = kb.bump("SP")
                    wait("vector", "SP", gm)
                    def ef(e, bank=bank, a=a, tp=tp):
                        dst = Xt[:, :].rearrange("p (j t) -> p j t", j=DPT)[
                            :, a * 8:(a + 1) * 8, tp * 128:(tp + 1) * 128]
                        e.tensor_copy(dst, pb[bank][:, :].bitcast(BF16)[:, 0:1024]
                                      .rearrange("p (j t) -> p j t", j=8)).then_inc(sem["SV"], 1)
                    ev("vector", ef)
                    bank_ev[bank] = kb.bump("SV")
                tp_m[tp] = kb.cnt["SP"]
        x_ready = kb.cnt["SV"]

        def emit_stats(x_rdy):
            wait("scalar", "SV", x_rdy)
            wait("tensor", "SV", x_rdy)
            sq_m = [0] * DPT
            s2_m = [0] * DPT
            for j in range(DPT):
                b = j % 2
                if j >= 2:
                    wait("scalar", "SP", s2_m[j - 2])
                ev("scalar", lambda e, j=j, b=b: e.activation(
                    out=sqt[b][:, :], in_=Xt[:, j * NT:(j + 1) * NT],
                    func=AF.Square).then_inc(sem["SS"], 1))
                sq_m[j] = kb.bump("SS")
                for ncc in range(2):
                    ev("tensor", lambda e, j=j, ncc=ncc: e.matmul(
                        pb[ncc][0:1, :], onesb[:, :],
                        Xt[:, j * NT + ncc * 512: j * NT + (ncc + 1) * 512],
                        start=(j == 0), stop=(j == DPT - 1)))
                wait("tensor", "SS", sq_m[j])
                for ncc in range(2):
                    last = (ncc == 1)
                    def f(e, j=j, b=b, ncc=ncc, last=last):
                        ins = e.matmul(pb[2 + ncc][0:1, :], onesb[:, :],
                                       sqt[b][:, ncc * 512:(ncc + 1) * 512],
                                       start=(j == 0), stop=(j == DPT - 1))
                        if last:
                            ins.then_inc(sem["SP"], 1)
                    ev("tensor", f)
                s2_m[j] = kb.bump("SP")
            mm_done = s2_m[DPT - 1]

            wait("vector", "SP", mm_done)
            ev("vector", lambda e: e.tensor_copy(s1row[0:1, 0:512], pb[0][0:1, :]))
            ev("vector", lambda e: e.tensor_copy(s1row[0:1, 512:1024], pb[1][0:1, :]))
            ev("vector", lambda e: e.tensor_copy(s2row[0:1, 0:512], pb[2][0:1, :]))
            ev("vector", lambda e: e.tensor_copy(s2row[0:1, 512:1024], pb[3][0:1, :]).then_inc(sem["SV"], 1))
            m1 = kb.bump("SV")
            wait("vector", "SV", m1)
            ev("vector", lambda e: e.tensor_scalar_mul(murow[0:1, :], s1row[0:1, :], 1.0 / D).then_inc(sem["SV"], 1))
            m2 = kb.bump("SV")
            wait("vector", "SV", m2)
            ev("vector", lambda e: e.tensor_mul(varow[0:1, :], murow[0:1, :], murow[0:1, :]).then_inc(sem["SV"], 1))
            m3 = kb.bump("SV")
            wait("vector", "SV", m3)
            ev("vector", lambda e: e.scalar_tensor_tensor(
                varow[0:1, :], s2row[0:1, :], 1.0 / D, varow[0:1, :],
                op0=ALU.mult, op1=ALU.subtract).then_inc(sem["SV"], 1))
            m4 = kb.bump("SV")
            wait("scalar", "SV", m4)
            ev("scalar", lambda e: e.activation(out=nrsrow[0:1, :], in_=varow[0:1, :],
                                                func=AF.Ln, bias=epst[0:1, :]).then_inc(sem["SS"], 1))
            m5 = kb.bump("SS")
            wait("scalar", "SS", m5)
            ev("scalar", lambda e: e.activation(out=nrsrow[0:1, :], in_=nrsrow[0:1, :],
                                                func=AF.Exp, scale=-0.5).then_inc(sem["SS"], 1))
            m6 = kb.bump("SS")
            wait("vector", "SS", m6)
            ev("vector", lambda e: e.tensor_scalar_mul(nrsrow[0:1, :], nrsrow[0:1, :], -1.0).then_inc(sem["SV"], 1))
            m7 = kb.bump("SV")
            wait("tensor", "SV", m7)
            for ncc in range(2):
                ev("tensor", lambda e, ncc=ncc: e.matmul(
                    pb[4 + ncc][:, :], ones1f[:, :], murow[0:1, ncc * 512:(ncc + 1) * 512],
                    start=True, stop=True))
                last = (ncc == 1)
                def f(e, ncc=ncc, last=last):
                    ins = e.matmul(pb[6 + ncc][:, :], ones1f[:, :],
                                   nrsrow[0:1, ncc * 512:(ncc + 1) * 512], start=True, stop=True)
                    if last:
                        ins.then_inc(sem["SP"], 1)
                ev("tensor", f)
            m8 = kb.bump("SP")
            wait("vector", "SP", m8)
            ev("vector", lambda e: e.tensor_copy(mub[:, 0:512], pb[4][:, :]))
            ev("vector", lambda e: e.tensor_copy(mub[:, 512:1024], pb[5][:, :]))
            ev("vector", lambda e: e.tensor_copy(nrstdb[:, 0:512], pb[6][:, :]))
            ev("vector", lambda e: e.tensor_copy(nrstdb[:, 512:1024], pb[7][:, :]).then_inc(sem["SV"], 1))
            return kb.bump("SV")

        def emit_ffn(pass_idx, x_rdy, force_first=False):
            stats_mark = emit_stats(x_rdy)
            first_pass = (pass_idx == 1) or force_first
            wait("gpsimd", "SV", x_rdy)
            wait("sync", "SV", x_rdy)

            with ExitStack() as ph:
                PE = ph.enter_context
                wrow_g = [PE(nc.sbuf_tensor(f"p{pass_idx}wg{i}", [128, 4096], BF16)) for i in range(2)]
                wrow_u = [PE(nc.sbuf_tensor(f"p{pass_idx}wu{i}", [128, 4096], BF16)) for i in range(2)]
                lhs_g = [PE(nc.sbuf_tensor(f"p{pass_idx}lg{i}", [128, 4096], BF16)) for i in range(2)]
                lhs_u = [PE(nc.sbuf_tensor(f"p{pass_idx}lu{i}", [128, 4096], BF16)) for i in range(2)]
                w00 = PE(nc.sbuf_tensor(f"p{pass_idx}w00", [128, 512], F32))
                w10 = PE(nc.sbuf_tensor(f"p{pass_idx}w10", [128, 512], F32))
                w01 = PE(nc.sbuf_tensor(f"p{pass_idx}w01", [128, 512], F32))
                w11 = PE(nc.sbuf_tensor(f"p{pass_idx}w11", [128, 512], F32))
                sl0 = PE(nc.sbuf_tensor(f"p{pass_idx}sl0", [128, 512], F32))
                sl1 = PE(nc.sbuf_tensor(f"p{pass_idx}sl1", [128, 512], F32))
                mst = [PE(nc.sbuf_tensor(f"p{pass_idx}ms{i}", [128, 1024], BF16)) for i in range(2)]

                ld_m = [0] * (HTN + 2)
                ldy_m = [0] * (HTN + 2)
                tp_done = [0] * (HTN + 1)
                evac_done = [0] * (HTN + 1)
                rs_done = [0] * (HTN + 1)
                wb_done = [0] * (HTN + 1)
                c1_done = [0] * (HTN + 1)
                mid_done = [0] * (HTN + 1)
                silu_done = [0] * (HTN + 1)
                mm_done = [0] * (HTN + 1)
                midwb = [0] * (HTN + 2)
                evgrp = [stats_mark, stats_mark]

                def emit_load(m):
                    b = m % 2
                    if first_pass:
                        if m >= 2:
                            wait("gpsimd", "SP", tp_done[m - 2])
                            wait("gpsimd", "SV", rs_done[m - 2])
                        ev("gpsimd", lambda e, m=m, b=b: e.dma_start(
                            out=wrow_g[b][:, :], in_=wg_d.ap()[m * 128:(m + 1) * 128, :]).then_inc(sem["LG"], 16))
                        kb.bump("LG", 16)
                        ev("gpsimd", lambda e, m=m, b=b: e.dma_start(
                            out=wrow_u[b][:, :], in_=wu_d.ap()[m * 128:(m + 1) * 128, :]).then_inc(sem["LG"], 16))
                        ld_m[m] = kb.bump("LG", 16)
                    else:
                        if m >= 2:
                            wait("sync", "SP", mm_done[m - 2])
                        ev("sync", lambda e, m=m, b=b: e.dma_start(
                            out=lhs_g[b][:, :], in_=wgT_s.ap()[m]).then_inc(sem["LY"], 16))
                        kb.bump("LY", 16)
                        ev("sync", lambda e, m=m, b=b: e.dma_start(
                            out=lhs_u[b][:, :], in_=wuT_s.ap()[m]).then_inc(sem["LY"], 16))
                        ldy_m[m] = kb.bump("LY", 16)

                def emit_transposes(m):
                    b = m % 2
                    wait("vector", "LG", kb.cnt["LG"])
                    ev("vector", lambda e, m=m, b=b: e.reduce_sum(
                        rs_g[:, m:m + 1], wrow_g[b][:, :], axis=AX.X))
                    ev("vector", lambda e, m=m, b=b: e.reduce_sum(
                        rs_u[:, m:m + 1], wrow_u[b][:, :], axis=AX.X).then_inc(sem["SV"], 1))
                    rs_done[m] = kb.bump("SV")
                    if m >= 2:
                        wait("vector", "WB", kb.cnt["WB"])
                    wait("tensor", "LG", kb.cnt["LG"])
                    for a in range(8):
                        bank = 4 + (a % 2)
                        src = wrow_g[b] if a < 4 else wrow_u[b]
                        joff = (a % 4) * 8
                        wait("tensor", "SV", evgrp[bank - 4])
                        for jj in range(8):
                            j = joff + jj
                            last = (jj == 7)
                            def tf(e, src=src, j=j, bank=bank, jj=jj, last=last):
                                ins = e.transpose(
                                    pb[bank][:, :].bitcast(BF16)[:, jj * 128:(jj + 1) * 128],
                                    src[:, j::32], ident[:, :])
                                if last:
                                    ins.then_inc(sem["SP"], 1)
                            ev("tensor", tf)
                        gm = kb.bump("SP")
                        dstbuf = lhs_g[b] if a < 4 else lhs_u[b]
                        wait("vector", "SP", gm)
                        def ef(e, dstbuf=dstbuf, bank=bank, a=a):
                            e.tensor_copy(dstbuf[:, (a % 4) * 1024:((a % 4) + 1) * 1024],
                                          pb[bank][:, :].bitcast(BF16)[:, 0:1024]).then_inc(sem["SV"], 1)
                        ev("vector", ef)
                        evgrp[bank - 4] = kb.bump("SV")
                    tp_done[m] = kb.cnt["SP"]
                    evac_done[m] = kb.cnt["SV"]
                    wait("sync", "SV", evac_done[m])
                    ev("sync", lambda e, m=m, b=b: e.dma_start(
                        out=wgT_s.ap()[m], in_=lhs_g[b][:, :]).then_inc(sem["WB"], 16))
                    kb.bump("WB", 16)
                    ev("sync", lambda e, m=m, b=b: e.dma_start(
                        out=wuT_s.ap()[m], in_=lhs_u[b][:, :]).then_inc(sem["WB"], 16))
                    wb_done[m] = kb.bump("WB", 16)

                def emit_mms(m):
                    b = m % 2
                    if first_pass:
                        wait("tensor", "SV", evac_done[m])
                    else:
                        wait("tensor", "LY", kb.cnt["LY"])
                        wait("tensor", "SV", stats_mark)
                    if m >= 1:
                        wait("tensor", "SV", c1_done[m - 1])
                    for kt in range(32):
                        for (lbuf, bank0) in ((lhs_g[b], 0), (lhs_u[b], 2)):
                            for ncc in range(2):
                                last = (kt == 31 and bank0 == 2 and ncc == 1)
                                def mf(e, lbuf=lbuf, kt=kt, bank0=bank0, ncc=ncc, last=last):
                                    ins = e.matmul(
                                        pb[bank0 + ncc][:, :],
                                        lbuf[:, kt * 128:(kt + 1) * 128],
                                        Xt[:, kt * NT + ncc * 512: kt * NT + (ncc + 1) * 512],
                                        start=(kt == 0), stop=(kt == 31))
                                    if last:
                                        ins.then_inc(sem["SP"], 1)
                                ev("tensor", mf)
                    mm_done[m] = kb.bump("SP")

                def emit_chain(m):
                    b = m % 2
                    wait("vector", "SP", mm_done[m])
                    if m >= 1:
                        wait("vector", "SS", silu_done[m - 1])
                    ev("vector", lambda e, m=m: e.scalar_tensor_tensor(
                        w00[:, :], mub[:, 0:512], rs_g[:, m:m + 1], pb[0][:, :],
                        op0=ALU.mult, op1=ALU.subtract))
                    ev("vector", lambda e, m=m: e.scalar_tensor_tensor(
                        w10[:, :], mub[:, 0:512], rs_u[:, m:m + 1], pb[2][:, :],
                        op0=ALU.mult, op1=ALU.subtract))
                    ev("vector", lambda e, m=m: e.scalar_tensor_tensor(
                        w01[:, :], mub[:, 512:1024], rs_g[:, m:m + 1], pb[1][:, :],
                        op0=ALU.mult, op1=ALU.subtract))
                    ev("vector", lambda e, m=m: e.scalar_tensor_tensor(
                        w11[:, :], mub[:, 512:1024], rs_u[:, m:m + 1], pb[3][:, :],
                        op0=ALU.mult, op1=ALU.subtract).then_inc(sem["SV"], 1))
                    c1_done[m] = kb.bump("SV")
                    wait("vector", "SV", c1_done[m])
                    ev("vector", lambda e: e.tensor_mul(w00[:, :], w00[:, :], nrstdb[:, 0:512]))
                    ev("vector", lambda e: e.scalar_tensor_tensor(
                        w10[:, :], w10[:, :], cu128[:, :], nrstdb[:, 0:512],
                        op0=ALU.mult, op1=ALU.mult))
                    ev("vector", lambda e: e.tensor_mul(w01[:, :], w01[:, :], nrstdb[:, 512:1024]))
                    ev("vector", lambda e: e.scalar_tensor_tensor(
                        w11[:, :], w11[:, :], cu128[:, :], nrstdb[:, 512:1024],
                        op0=ALU.mult, op1=ALU.mult).then_inc(sem["SV"], 1))
                    c2m = kb.bump("SV")
                    wait("scalar", "SV", c2m)
                    if m >= 2:
                        wait("scalar", "WB", kb.cnt["WB"])
                    ev("scalar", lambda e: e.activation(out=sl0[:, :], in_=w00[:, :],
                                                        func=AF.Silu, scale=sg128[:, :]))
                    ev("scalar", lambda e: e.activation(out=sl1[:, :], in_=w01[:, :],
                                                        func=AF.Silu, scale=sg128[:, :]).then_inc(sem["SS"], 1))
                    silu_done[m] = kb.bump("SS")
                    wait("vector", "SS", silu_done[m])
                    if m >= 2:
                        wait("vector", "WB", kb.cnt["WB"])
                    ev("vector", lambda e, b=b: e.tensor_mul(mst[b][:, 0:512], sl0[:, :], w10[:, :]))
                    ev("vector", lambda e, b=b: e.tensor_mul(mst[b][:, 512:1024], sl1[:, :], w11[:, :]).then_inc(sem["SV"], 1))
                    mid_done[m] = kb.bump("SV")
                    wait("sync", "SV", mid_done[m])
                    ev("sync", lambda e, m=m, b=b: e.dma_start(
                        out=mid_s.ap()[m], in_=mst[b][:, :]).then_inc(sem["WB"], 16))
                    midwb[m] = kb.bump("WB", 16)

                if first_pass:
                    emit_load(0)
                    emit_load(1)
                    emit_transposes(0)
                    for m in range(HTN):
                        if m + 2 < HTN:
                            emit_load(m + 2)
                        if m + 1 < HTN:
                            emit_transposes(m + 1)
                        emit_mms(m)
                        emit_chain(m)
                else:
                    emit_load(0)
                    emit_load(1)
                    for m in range(HTN):
                        emit_mms(m)
                        if m + 2 < HTN:
                            emit_load(m + 2)
                        emit_chain(m)

                guphase_wb = kb.cnt["WB"]

            with ExitStack() as ph:
                PE = ph.enter_context
                midin = PE(nc.sbuf_tensor(f"p{pass_idx}mi", [128, HTN * 512], BF16))
                wrow_d = [PE(nc.sbuf_tensor(f"p{pass_idx}wd{i}", [128, WCH * 128], BF16)) for i in range(2)]
                lhs_d = [PE(nc.sbuf_tensor(f"p{pass_idx}ldn{i}", [128, WCH * 128], BF16)) for i in range(2)]

                nwch = (HTN + WCH - 1) // WCH
                wchs = [min(WCH, HTN - i * WCH) for i in range(nwch)]
                seq = [(dpt, c) for dpt in range(DPT) for c in range(nwch)]

                dwb0 = {}
                dn_prev = [stats_mark, stats_mark]
                for ncc in range(2):
                    build_T = first_pass and ncc == 0
                    wait("sync", "WB", kb.cnt["WB"])
                    wait("sync", "SP", kb.cnt["SP"])
                    def mf(e, ncc=ncc):
                        src = mid_s.ap()[:, :, ncc * 512:(ncc + 1) * 512].transpose([1, 0, 2])
                        e.dma_start(out=midin[:, :].rearrange("p (k t) -> p k t", k=HTN),
                                    in_=src).then_inc(sem["LY"], 16)
                    ev("sync", mf)
                    mid_ld = kb.bump("LY", 16)

                    dl_mark = {}
                    dl_evac = {}
                    dtp_done = {}
                    dch_done = {}
                    dn_upd = [0] * (DPT + 1)
                    devgrp = [kb.cnt["SV"], kb.cnt["SV"]]
                    ncc_entry_sp = kb.cnt["SP"]

                    def emit_dload(i):
                        dpt, c = seq[i]
                        b = i % 2
                        nk = wchs[c]
                        if build_T:
                            if i >= 2:
                                wait("gpsimd", "SP", dtp_done.get(seq[i - 2], 0))
                            else:
                                wait("gpsimd", "SP", ncc_entry_sp)
                            def f(e, dpt=dpt, c=c, b=b, nk=nk):
                                src = wd_d.ap().rearrange("(r j) h -> j r h", j=32)[dpt][
                                    :, c * WCH * 128: c * WCH * 128 + nk * 128]
                                e.dma_start(out=wrow_d[b][:, 0:nk * 128], in_=src).then_inc(sem["LG"], 16)
                            ev("gpsimd", f)
                            dl_mark[(dpt, c)] = kb.bump("LG", 16)
                        else:
                            if i >= 2:
                                wait("sync", "SP", dch_done.get(seq[i - 2], 0))
                            else:
                                wait("sync", "SP", ncc_entry_sp)
                            if first_pass:
                                wait("sync", "WB", kb.cnt["WB"])
                            def f(e, dpt=dpt, c=c, b=b, nk=nk):
                                src = wdT_s.ap()[dpt, c * WCH:c * WCH + nk].transpose([1, 0, 2])
                                e.dma_start(out=lhs_d[b][:, 0:nk * 128].rearrange("p (k f) -> p k f", k=nk),
                                            in_=src).then_inc(sem["LY"], 16)
                            ev("sync", f)
                            dl_mark[(dpt, c)] = kb.bump("LY", 16)

                    def emit_dtrans(i):
                        dpt, c = seq[i]
                        b = i % 2
                        nk = wchs[c]
                        wait("tensor", "LG", kb.cnt["LG"])
                        ngr = (nk + 7) // 8
                        for a in range(ngr):
                            cnt = min(8, nk - a * 8)
                            bank = 4 + (a % 2)
                            wait("tensor", "SV", devgrp[bank - 4])
                            for jj in range(cnt):
                                kt = a * 8 + jj
                                last = (jj == cnt - 1)
                                def tf(e, b=b, kt=kt, bank=bank, jj=jj, last=last):
                                    ins = e.transpose(
                                        pb[bank][:, :].bitcast(BF16)[:, jj * 128:(jj + 1) * 128],
                                        wrow_d[b][:, kt * 128:(kt + 1) * 128], ident[:, :])
                                    if last:
                                        ins.then_inc(sem["SP"], 1)
                                ev("tensor", tf)
                            gm = kb.bump("SP")
                            wait("vector", "SP", gm)
                            def ef(e, b=b, bank=bank, a=a, cnt=cnt):
                                e.tensor_copy(lhs_d[b][:, a * 1024:a * 1024 + cnt * 128],
                                              pb[bank][:, :].bitcast(BF16)[:, 0:cnt * 128]).then_inc(sem["SV"], 1)
                            ev("vector", ef)
                            devgrp[bank - 4] = kb.bump("SV")
                        dtp_done[(dpt, c)] = kb.cnt["SP"]
                        dl_evac[(dpt, c)] = kb.cnt["SV"]
                        wait("sync", "SV", dl_evac[(dpt, c)])
                        def wf(e, dpt=dpt, c=c, b=b, nk=nk):
                            dst = wdT_s.ap()[dpt, c * WCH:c * WCH + nk].transpose([1, 0, 2])
                            e.dma_start(out=dst, in_=lhs_d[b][:, 0:nk * 128]
                                        .rearrange("p (k f) -> p k f", k=nk)).then_inc(sem["WB"], 16)
                        ev("sync", wf)
                        dwb0[(dpt, c)] = kb.bump("WB", 16)

                    wait("tensor", "LY", mid_ld)
                    emit_dload(0)
                    for i, (dpt, c) in enumerate(seq):
                        b = i % 2
                        nk = wchs[c]
                        if i + 1 < len(seq):
                            emit_dload(i + 1)
                        if build_T:
                            emit_dtrans(i)
                            wait("tensor", "SV", dl_evac[(dpt, c)])
                        else:
                            wait("tensor", "LY", kb.cnt["LY"])
                        bank = 6 + (dpt % 2)
                        if c == 0:
                            if dpt >= 2:
                                wait("tensor", "SV", dn_upd[dpt - 2])
                            else:
                                wait("tensor", "SV", dn_prev[dpt])
                        for kt in range(nk):
                            gk = c * WCH + kt
                            last = (kt == nk - 1)
                            def mmf(e, b=b, kt=kt, gk=gk, bank=bank, last=last):
                                ins = e.matmul(
                                    pb[bank][:, :],
                                    lhs_d[b][:, kt * 128:(kt + 1) * 128],
                                    midin[:, gk * 512:(gk + 1) * 512],
                                    start=(gk == 0), stop=(gk == HTN - 1))
                                if last:
                                    ins.then_inc(sem["SP"], 1)
                            ev("tensor", mmf)
                        dch_done[(dpt, c)] = kb.bump("SP")
                        if c == nwch - 1:
                            wait("vector", "SP", dch_done[(dpt, c)])
                            def uf(e, dpt=dpt, bank=bank, ncc=ncc):
                                dst = Xt[:, dpt * NT + ncc * 512: dpt * NT + (ncc + 1) * 512]
                                e.tensor_add(dst, dst, pb[bank][:, :]).then_inc(sem["SV"], 1)
                            ev("vector", uf)
                            dn_upd[dpt] = kb.bump("SV")
                    dn_prev = [dn_upd[DPT - 2], dn_upd[DPT - 1]]
            return kb.cnt["SV"]

        def emit_attn(x_rdy):
            stats_mark = emit_stats(x_rdy)
            wait("gpsimd", "SV", x_rdy)
            wait("sync", "SV", x_rdy)

            with ExitStack() as ph:
                PE = ph.enter_context
                Ysub = PE(nc.sbuf_tensor("at_y", [128, DPT * 512], BF16))
                ytmp = [PE(nc.sbuf_tensor(f"at_yt{i}", [128, 512], F32)) for i in range(2)]
                qraw = [PE(nc.sbuf_tensor(f"at_qr{i}", [128, 4096], BF16)) for i in range(2)]
                kraw = [PE(nc.sbuf_tensor(f"at_kr{i}", [128, 1024], BF16)) for i in range(2)]
                vraw = [PE(nc.sbuf_tensor(f"at_vr{i}", [128, 1024], BF16)) for i in range(2)]
                wtq = [PE(nc.sbuf_tensor(f"at_wq{i}", [128, 1024], BF16)) for i in range(2)]
                Qph = PE(nc.sbuf_tensor("at_Q", [128, 512 * 8], BF16))
                Kph = PE(nc.sbuf_tensor("at_K", [128, 512 * 8], BF16))
                Vph = PE(nc.sbuf_tensor("at_V", [128, 512 * 9], BF16))
                qst = PE(nc.sbuf_tensor("at_qs", [128, 96], BF16))
                kst = PE(nc.sbuf_tensor("at_ks", [128, 96], BF16))
                qTs = PE(nc.sbuf_tensor("at_qT", [128, 128], BF16))
                kTs = PE(nc.sbuf_tensor("at_kT", [128, 128], BF16))
                m3 = PE(nc.sbuf_tensor("at_m3", [128, 3], F32))
                Esb = PE(nc.sbuf_tensor("at_E", [128, 384], BF16))
                ETs = PE(nc.sbuf_tensor("at_ET", [128, 384], BF16))
                Oph = PE(nc.sbuf_tensor("at_O", [128, 8 * 512], BF16))
                Rde = PE(nc.sbuf_tensor("at_R", [128, 512], F32))
                OR2 = PE(nc.sbuf_tensor("at_OR", [128, 512], F32))

                for half in range(2):
                    hoff = half * 512
                    wait("vector", "SV", stats_mark)
                    for j in range(DPT):
                        yb = j % 2
                        def yf1(e, j=j, hoff=hoff, yb=yb):
                            e.tensor_sub(ytmp[yb][:, :], mub[:, hoff:hoff + 512],
                                         Xt[:, j * NT + hoff:j * NT + hoff + 512]).then_inc(sem["SV"], 1)
                        ev("vector", yf1)
                        t1 = kb.bump("SV")
                        wait("vector", "SV", t1)
                        def yf2(e, j=j, hoff=hoff, yb=yb):
                            e.tensor_mul(Ysub[:, j * 512:(j + 1) * 512], ytmp[yb][:, :],
                                         nrstdb[:, hoff:hoff + 512]).then_inc(sem["SV"], 1)
                        ev("vector", yf2)
                        kb.bump("SV")
                    y_ready = kb.cnt["SV"]

                    ev("vector", lambda e: e.memset(
                        Vph[:, :].rearrange("p (t c) -> p t c", t=512)[:, :, 8:9], 1.0).then_inc(sem["SV"], 1))
                    ones_mark = kb.bump("SV")

                    sub_ev_done = kb.cnt["SV"]
                    for sub, (wsrc_d, rbuf, wid, dstp) in enumerate((
                            (wq_d, qraw, 4096, Qph),
                            (wk_d, kraw, 1024, Kph),
                            (wv_d, vraw, 1024, Vph))):
                        entry_sp = kb.cnt["SP"]
                        ld_marks = {}
                        kt_mm = {}
                        wt_marks = {}
                        nh = wid // 128

                        def emit_qkv_load(kt):
                            b = kt % 2
                            if kt >= 2:
                                wait("gpsimd", "SP", kt_mm[kt - 2])
                            else:
                                wait("gpsimd", "SP", entry_sp)
                            def lf(e, kt=kt, b=b, wsrc_d=wsrc_d, rbuf=rbuf, wid=wid, nh=nh):
                                src = wsrc_d.ap().rearrange("(r j) d h -> j r d h", j=32)[kt]
                                dst = rbuf[b][:, 0:wid].rearrange("p (d h) -> p d h", h=nh)
                                e.dma_start(out=dst, in_=src).then_inc(sem["LG"], 16)
                            ev("gpsimd", lf)
                            ld_marks[kt] = kb.bump("LG", 16)

                        emit_qkv_load(0)
                        emit_qkv_load(1)
                        for kt in range(32):
                            b = kt % 2
                            if sub == 0:
                                wait("vector", "LG", kb.cnt["LG"])
                                if kt >= 2:
                                    wait("vector", "SP", kt_mm[kt - 2])
                                def bf(e, b=b):
                                    src = qraw[b][:, :].rearrange("p (d g j) -> p d g j", g=8, j=4) \
                                        .transpose([0, 2, 1, 3])
                                    with nc.allow_low_precision(reason="bf16 store of fp32-internal 4-term weight group sum"):
                                        e.reduce_sum(
                                            wtq[b][:, :].rearrange("p (g d) -> p g d", g=8),
                                            src, axis=AX.X).then_inc(sem["SV"], 1)
                                ev("vector", bf)
                                wt_marks[kt] = kb.bump("SV")
                                wait("tensor", "SV", wt_marks[kt])
                                lsrc = wtq[b]
                            else:
                                wait("tensor", "LG", kb.cnt["LG"])
                                lsrc = rbuf[b]
                            if kt == 0:
                                wait("tensor", "SV", y_ready)
                                wait("tensor", "SV", sub_ev_done)
                            for g in range(8):
                                if sub == 0:
                                    lap = lsrc[:, g * 128:(g + 1) * 128]
                                else:
                                    lap = lsrc[:, g:wid:8]
                                last = (g == 7)
                                def mfq(e, lap=lap, g=g, kt=kt, last=last):
                                    ins = e.matmul(pb[g][:, :], lap,
                                                   Ysub[:, kt * 512:(kt + 1) * 512],
                                                   start=(kt == 0), stop=(kt == 31))
                                    if last:
                                        ins.then_inc(sem["SP"], 1)
                                ev("tensor", mfq)
                            kt_mm[kt] = kb.bump("SP")
                            if kt + 2 < 32:
                                emit_qkv_load(kt + 2)
                        wait("vector", "SP", kt_mm[31])
                        if sub == 2:
                            wait("vector", "SV", ones_mark)
                        for g in range(8):
                            last = (g == 7)
                            def ef(e, sub=sub, g=g, dstp=dstp, last=last):
                                nbl = 9 if sub == 2 else 8
                                dst = dstp[:, :].rearrange("p (t c) -> p t c", t=512)[:, :, g:g + 1].squeeze(2)
                                if sub == 0:
                                    ins = e.tensor_scalar(dst, pb[g][:, :], btq[:, g:g + 1], RS128,
                                                          op0=ALU.add, op1=ALU.mult)
                                elif sub == 1:
                                    ins = e.tensor_scalar_add(dst, pb[g][:, :], bkv[:, g:g + 1])
                                else:
                                    ins = e.tensor_scalar_add(dst, pb[g][:, :], bkv[:, 8 + g:9 + g])
                                if last:
                                    ins.then_inc(sem["SV"], 1)
                            ev("vector", ef)
                        sub_ev_done = kb.bump("SV")
                    qkv_done = sub_ev_done

                    ngroups = (512 + 2) // 3
                    mk = {"stage": 0, "tev": 0, "red": 0, "etev": 0, "ocp": 0,
                          "exp": kb.cnt["SS"], "tp": 0, "sc": 0, "et": 0, "om": 0}
                    wait("vector", "SV", qkv_done)
                    wait("tensor", "SV", qkv_done)
                    for gi in range(ngroups):
                        t0 = gi * 3
                        ng = min(3, 512 - t0)
                        if gi > 0:
                            wait("vector", "SP", mk["tp"])
                        def sf(e, t0=t0, ng=ng):
                            e.tensor_copy(
                                qst[:, 0:ng * 32].rearrange("p (t c) -> p t c", t=ng)[:, :, 0:8],
                                Qph[:, t0 * 8:(t0 + ng) * 8].rearrange("p (t c) -> p t c", t=ng))
                            e.tensor_copy(
                                kst[:, 0:ng * 32].rearrange("p (t c) -> p t c", t=ng)[:, :, 0:8],
                                Kph[:, t0 * 8:(t0 + ng) * 8].rearrange("p (t c) -> p t c", t=ng)).then_inc(sem["SV"], 1)
                        ev("vector", sf)
                        mk["stage"] = kb.bump("SV")
                        wait("tensor", "SV", mk["stage"])
                        if gi > 0:
                            wait("tensor", "SV", mk["tev"])
                        def tf(e, ng=ng):
                            e.transpose(pb[0][0:32 * ng, 0:64].bitcast(BF16), qst[:, 0:ng * 32], ident[:, :])
                            e.transpose(pb[1][0:32 * ng, 0:64].bitcast(BF16), kst[:, 0:ng * 32],
                                        ident[:, :]).then_inc(sem["SP"], 1)
                        ev("tensor", tf)
                        mk["tp"] = kb.bump("SP")
                        wait("vector", "SP", mk["tp"])
                        def vf(e, ng=ng):
                            e.tensor_copy(qTs[0:32 * ng, :], pb[0][0:32 * ng, 0:64].bitcast(BF16))
                            e.tensor_copy(kTs[0:32 * ng, :], pb[1][0:32 * ng, 0:64].bitcast(BF16)).then_inc(sem["SV"], 1)
                        ev("vector", vf)
                        mk["tev"] = kb.bump("SV")
                        wait("tensor", "SV", mk["tev"])
                        wait("tensor", "SS", mk["exp"])
                        if gi > 0:
                            wait("tensor", "SV", mk["etev"])
                        for i in range(ng):
                            last = (i == ng - 1)
                            def scf(e, i=i, last=last):
                                ins = e.matmul(pb[2 + i][:, 0:128],
                                               qTs[32 * i:32 * i + 8, :], kTs[32 * i:32 * i + 8, :],
                                               start=True, stop=True, tile_position=(32 * i, 0))
                                if last:
                                    ins.then_inc(sem["SP"], 1)
                            ev("tensor", scf)
                        mk["sc"] = kb.bump("SP")
                        wait("vector", "SP", mk["sc"])
                        wait("vector", "SS", mk["exp"])
                        for i in range(ng):
                            last = (i == ng - 1)
                            def rf(e, i=i, last=last):
                                ins = e.reduce_max(m3[:, i:i + 1], pb[2 + i][:, 0:128],
                                                   axis=AX.X, negate=True)
                                if last:
                                    ins.then_inc(sem["SV"], 1)
                            ev("vector", rf)
                        mk["red"] = kb.bump("SV")
                        wait("scalar", "SV", mk["red"])
                        if gi > 0:
                            wait("scalar", "SP", mk["et"])
                        for i in range(ng):
                            last = (i == ng - 1)
                            def xf(e, i=i, last=last):
                                ins = e.activation(out=Esb[:, i * 128:(i + 1) * 128],
                                                   in_=pb[2 + i][:, 0:128], func=AF.Exp,
                                                   bias=m3[:, i:i + 1])
                                if last:
                                    ins.then_inc(sem["SS"], 1)
                            ev("scalar", xf)
                        mk["exp"] = kb.bump("SS")
                        wait("tensor", "SS", mk["exp"])
                        for i in range(ng):
                            last = (i == ng - 1)
                            def etf(e, i=i, last=last):
                                ins = e.transpose(pb[2 + i][:, 0:64].bitcast(BF16),
                                                  Esb[:, i * 128:(i + 1) * 128], ident[:, :])
                                if last:
                                    ins.then_inc(sem["SP"], 1)
                            ev("tensor", etf)
                        mk["et"] = kb.bump("SP")
                        wait("vector", "SP", mk["et"])
                        for i in range(ng):
                            last = (i == ng - 1)
                            def eef(e, i=i, last=last):
                                ins = e.tensor_copy(ETs[:, i * 128:(i + 1) * 128],
                                                    pb[2 + i][:, 0:64].bitcast(BF16))
                                if last:
                                    ins.then_inc(sem["SV"], 1)
                            ev("vector", eef)
                        mk["etev"] = kb.bump("SV")
                        wait("tensor", "SV", mk["etev"])
                        if gi > 0:
                            wait("tensor", "SV", mk["ocp"])
                        for i in range(ng):
                            t = t0 + i
                            last = (i == ng - 1)
                            def omf(e, i=i, t=t, last=last):
                                ins = e.matmul(pb[5][:, 9 * i:9 * i + 9],
                                               ETs[:, i * 128:(i + 1) * 128],
                                               Vph[:, 9 * t:9 * t + 9],
                                               start=True, stop=True)
                                if last:
                                    ins.then_inc(sem["SP"], 1)
                            ev("tensor", omf)
                        mk["om"] = kb.bump("SP")
                        wait("vector", "SP", mk["om"])
                        def cf(e, t0=t0, ng=ng):
                            src = pb[5][:, 0:9 * ng].rearrange("p (i c) -> p i c", i=ng)
                            dstO = Oph[:, :].rearrange("p (g t) -> p g t", g=8)[:, :, t0:t0 + ng] \
                                .transpose([0, 2, 1])
                            e.tensor_copy(dstO, src[:, :, 0:8])
                            e.tensor_copy(Rde[:, t0:t0 + ng], src[:, :, 8:9].squeeze(2)).then_inc(sem["SV"], 1)
                        ev("vector", cf)
                        mk["ocp"] = kb.bump("SV")

                    wait("vector", "SV", mk["ocp"])
                    ev("vector", lambda e: e.reciprocal(Rde[:, :], Rde[:, :]).then_inc(sem["SV"], 1))
                    rr_m = kb.bump("SV")
                    wait("vector", "SV", rr_m)
                    for g in range(8):
                        def of(e, g=g):
                            e.tensor_mul(OR2[:, :], Oph[:, g * 512:(g + 1) * 512], Rde[:, :]).then_inc(sem["SV"], 1)
                        ev("vector", of)
                        orm = kb.bump("SV")
                        wait("vector", "SV", orm)
                        for j in range(4):
                            pt = 4 * g + j
                            last = (j == 3)
                            def af(e, pt=pt, hoff=hoff, last=last):
                                dst = Xt[:, pt * NT + hoff: pt * NT + hoff + 512]
                                ins = e.tensor_add(dst, dst, OR2[:, :])
                                if last:
                                    ins.then_inc(sem["SV"], 1)
                            ev("vector", af)
                        kb.bump("SV")
            return kb.cnt["SV"]

        def emit_output(x_rdy):
            with ExitStack() as ph:
                xout = [ph.enter_context(nc.sbuf_tensor(f"xo{i}", [128, 4096], F32)) for i in range(2)]
                wait("tensor", "SV", x_rdy)
                st_m = [0] * 8
                bank_ev = {0: kb.cnt["SV"], 1: kb.cnt["SV"]}
                for tp in range(8):
                    b = tp % 2
                    for a in range(4):
                        bank = a % 2
                        wait("tensor", "SV", bank_ev[bank])
                        for jj in range(8):
                            j = a * 8 + jj
                            last = (jj == 7)
                            def tf(e, j=j, tp=tp, bank=bank, jj=jj, last=last):
                                ins = e.transpose(
                                    pb[bank][:, :].bitcast(BF16)[:, jj * 128:(jj + 1) * 128],
                                    Xt[:, j * NT + tp * 128: j * NT + (tp + 1) * 128], ident[:, :])
                                if last:
                                    ins.then_inc(sem["SP"], 1)
                            ev("tensor", tf)
                        gm = kb.bump("SP")
                        wait("vector", "SP", gm)
                        if a == 0 and tp >= 2:
                            wait("vector", "SO", kb.cnt["SO"])
                        def ef(e, bank=bank, a=a, b=b):
                            src = pb[bank][:, :].bitcast(BF16)[:, 0:1024].rearrange("p (j t) -> p j t", j=8)
                            dst = xout[b][:, :].rearrange("p (t j) -> p t j", j=32)[:, :, a * 8:(a + 1) * 8] \
                                .transpose([0, 2, 1])
                            e.tensor_copy(dst, src).then_inc(sem["SV"], 1)
                        ev("vector", ef)
                        bank_ev[bank] = kb.bump("SV")
                    evdone = kb.cnt["SV"]
                    wait("sync", "SV", evdone)
                    ev("sync", lambda e, tp=tp, b=b: e.dma_start(
                        out=out_d.ap()[tp * 128:(tp + 1) * 128, :], in_=xout[b][:, :]).then_inc(sem["SO"], 16))
                    st_m[tp] = kb.bump("SO", 16)
                if dbg2_d is not None:
                    wait("gpsimd", "SV", x_rdy)
                    ev("gpsimd", lambda e: e.dma_start(
                        out=dbg2_d.ap()[:, 2560:3584], in_=mid_s.ap()[0]).then_inc(sem["SO"], 16))
                    kb.bump("SO", 16)
                    ev("gpsimd", lambda e: e.dma_start(
                        out=dbg2_d.ap()[:, 3584:4096], in_=wgT_s.ap()[0][:, 0:512]).then_inc(sem["SO"], 16))
                    kb.bump("SO", 16)
                    for dst_sl, src_ap in (
                            ((slice(0, 128), slice(0, 1024)), mub[:, :]),
                            ((slice(0, 128), slice(1024, 2048)), nrstdb[:, :]),
                            ((slice(0, 128), slice(2048, 2048 + HTN)), rs_g[:, :]),
                            ((slice(0, 128), slice(2176, 2176 + HTN)), rs_u[:, :]),
                            ((slice(0, 1), slice(4096, 5120)), murow[0:1, :]),
                            ((slice(0, 1), slice(5120, 6144)), nrsrow[0:1, :]),
                            ((slice(0, 1), slice(6144, 7168)), s1row[0:1, :]),
                            ((slice(0, 1), slice(7168, 8192)), s2row[0:1, :])):
                        ev("sync", lambda e, dst_sl=dst_sl, src_ap=src_ap: e.dma_start(
                            out=dbg2_d.ap()[dst_sl[0], dst_sl[1]], in_=src_ap).then_inc(sem["SO"], 16))
                        kb.bump("SO", 16)
                for name, dd in dbg.items():
                    for j in range(DPT):
                        wait("vector", "SO", kb.cnt["SO"])
                        def df(e, j=j):
                            e.tensor_copy(xout[0][:, 0:1024], Xt[:, j * NT:(j + 1) * NT]).then_inc(sem["SV"], 1)
                        ev("vector", df)
                        dm = kb.bump("SV")
                        wait("sync", "SV", dm)
                        ev("sync", lambda e, dd=dd, j=j: e.dma_start(
                            out=dd.ap()[:, j * NT:(j + 1) * NT], in_=xout[0][:, 0:1024]).then_inc(sem["SO"], 16))
                        kb.bump("SO", 16)
                wait("sync", "SO", kb.cnt["SO"])

        cur = x_ready
        if "ffn1" in sub_layers:
            cur = emit_ffn(1, cur)
        if "attn" in sub_layers:
            cur = emit_attn(cur)
        if "ffn2" in sub_layers:
            cur = emit_ffn(3, cur)
        if "ffn2f" in sub_layers:
            cur = emit_ffn(3, cur, force_first=True)
        emit_output(cur)

        @block.sync
        def _(e):
            for f in kb.ops["sync"]:
                f(e)

        @block.gpsimd
        def _(e):
            for f in kb.ops["gpsimd"]:
                f(e)

        @block.vector
        def _(e):
            for f in kb.ops["vector"]:
                f(e)

        @block.scalar
        def _(e):
            for f in kb.ops["scalar"]:
                f(e)

        @block.tensor
        def _(e):
            for f in kb.ops["tensor"]:
                f(e)

    return nc


_NC_CACHE = {}


def kernel(L0_attn_x, L0_attn_Wq, L0_attn_bq, L0_attn_Wk, L0_attn_bk,
           L0_attn_Wv, L0_attn_bv, L0_ffn_w_gate, L0_ffn_w_up, L0_ffn_w_down,
           L0_ffn_scale_gate, L0_ffn_scale_up, L0_ffn_scale_down,
           _debug_dump=(), _sub_layers=("ffn1", "attn", "ffn2")):
    x = np.ascontiguousarray(np.asarray(L0_attn_x).reshape(8192, 4096))
    wq = np.ascontiguousarray(np.asarray(L0_attn_Wq))
    wk = np.ascontiguousarray(np.asarray(L0_attn_Wk))
    wv = np.ascontiguousarray(np.asarray(L0_attn_Wv))
    bq = np.ascontiguousarray(np.asarray(L0_attn_bq))
    bk = np.ascontiguousarray(np.asarray(L0_attn_bk))
    bv = np.ascontiguousarray(np.asarray(L0_attn_bv))
    wg = np.ascontiguousarray(np.asarray(L0_ffn_w_gate))
    wu = np.ascontiguousarray(np.asarray(L0_ffn_w_up))
    wd = np.ascontiguousarray(np.asarray(L0_ffn_w_down))
    sg = np.asarray(L0_ffn_scale_gate, dtype=np.float32).reshape(1, 1)
    su = np.asarray(L0_ffn_scale_up, dtype=np.float32).reshape(1, 1)
    sd = np.asarray(L0_ffn_scale_down, dtype=np.float32).reshape(1, 1)
    idm = np.eye(128, dtype=ml_dtypes.bfloat16)

    key = (tuple(_debug_dump), tuple(_sub_layers))
    if key not in _NC_CACHE:
        _NC_CACHE[key] = build_nc(debug_dump=_debug_dump, sub_layers=_sub_layers)
    nc = _NC_CACHE[key]

    in_maps = []
    for c in range(8):
        in_maps.append({
            "x": x[c * 1024:(c + 1) * 1024],
            "wq": wq, "bq": bq, "wk": wk, "bk": bk, "wv": wv, "bv": bv,
            "wg": wg, "wu": wu, "wd": wd,
            "sg": sg, "su": su, "sd": sd, "idm": idm,
        })
    res = run_bass_kernel_spmd(nc, in_maps, core_ids=list(range(8)))
    out = np.concatenate([r["out"] for r in res.results], axis=0)
    if _debug_dump:
        return out.reshape(4, 2048, 4096), [
            {k: v for k, v in r.items() if k.startswith("dbg_")} for r in res.results]
    return out.reshape(4, 2048, 4096)


# version 05
# speedup vs baseline: 1.5440x; 1.5440x over previous
import math
import numpy as np
import ml_dtypes
from contextlib import ExitStack

import concourse.bass as bass
import concourse.mybir as mybir
from concourse.bass_utils import run_bass_kernel_spmd

F32 = mybir.dt.float32
BF16 = mybir.dt.bfloat16
I32 = mybir.dt.int32
AF = mybir.ActivationFunctionType
ALU = mybir.AluOpType
AX = mybir.AxisListType

NT = 1024
D = 4096
DPT = 32
H = 11008
HTN = 86
EPS = 1e-5
RS128 = 1.0 / math.sqrt(128.0)
WCH = 16

ENGS = ("sync", "gpsimd", "vector", "scalar", "tensor")

SEMS = ("SP", "SV", "SS", "LC", "LY0", "LY1", "LG0", "LG1", "MI",
        "WB0", "WB1", "WM0", "WM1", "DW0", "DW1", "SO0", "SO1")


class KB:

    def __init__(self):
        self.ops = {e: [] for e in ENGS}
        self.cnt = {s: 0 for s in SEMS}

    def bump(self, s, k=1):
        self.cnt[s] += k
        return self.cnt[s]


def build_nc(debug_dump=(), sub_layers=("ffn1", "attn", "ffn2")):
    nc = bass.Bass("TRN2", target_bir_lowering=False)

    x_d = nc.dram_tensor("x", [NT, D], F32, kind="ExternalInput")
    wq_d = nc.dram_tensor("wq", [D, 128, 32], F32, kind="ExternalInput")
    bq_d = nc.dram_tensor("bq", [128, 32], F32, kind="ExternalInput")
    wk_d = nc.dram_tensor("wk", [D, 128, 8], F32, kind="ExternalInput")
    bk_d = nc.dram_tensor("bk", [128, 8], F32, kind="ExternalInput")
    wv_d = nc.dram_tensor("wv", [D, 128, 8], F32, kind="ExternalInput")
    bv_d = nc.dram_tensor("bv", [128, 8], F32, kind="ExternalInput")
    wg_d = nc.dram_tensor("wg", [H, D], I32, kind="ExternalInput")
    wu_d = nc.dram_tensor("wu", [H, D], I32, kind="ExternalInput")
    wd_d = nc.dram_tensor("wd", [D, H], I32, kind="ExternalInput")
    sg_d = nc.dram_tensor("sg", [1, 1], F32, kind="ExternalInput")
    su_d = nc.dram_tensor("su", [1, 1], F32, kind="ExternalInput")
    sd_d = nc.dram_tensor("sd", [1, 1], F32, kind="ExternalInput")
    id_d = nc.dram_tensor("idm", [128, 128], BF16, kind="ExternalInput")

    out_d = nc.dram_tensor("out", [NT, D], F32, kind="ExternalOutput")
    dbg = {}
    dbg2_d = None
    for name in debug_dump:
        if name == "small":
            dbg2_d = nc.dram_tensor("dbg_small", [128, 8192], F32, kind="ExternalOutput")
            continue
        dbg[name] = nc.dram_tensor("dbg_" + name, [128, DPT * NT], F32,
                                   kind="ExternalOutput")

    wgT_s = nc.dram_tensor("wgT_s", [HTN, 128, 4096], BF16)
    wuT_s = nc.dram_tensor("wuT_s", [HTN, 128, 4096], BF16)
    wdT_s = nc.dram_tensor("wdT_s", [DPT, HTN, 128, 128], BF16)
    mid_s = nc.dram_tensor("mid_s", [HTN, 128, NT], BF16)

    kb = KB()

    with ExitStack() as stk:
        E = stk.enter_context
        Xt = E(nc.sbuf_tensor("Xt", [128, DPT * NT], BF16))
        ident = E(nc.sbuf_tensor("ident", [128, 128], BF16))
        onesb = E(nc.sbuf_tensor("onesb", [128, 1], BF16))
        ones1f = E(nc.sbuf_tensor("ones1f", [1, 128], F32))
        epst = E(nc.sbuf_tensor("epst", [1, 1], F32))
        mub = E(nc.sbuf_tensor("mub", [128, NT], F32))
        nrstdb = E(nc.sbuf_tensor("nrstdb", [128, NT], F32))
        s1row = E(nc.sbuf_tensor("s1row", [1, NT], F32))
        s2row = E(nc.sbuf_tensor("s2row", [1, NT], F32))
        murow = E(nc.sbuf_tensor("murow", [1, NT], F32))
        varow = E(nc.sbuf_tensor("varow", [1, NT], F32))
        nrsrow = E(nc.sbuf_tensor("nrsrow", [1, NT], F32))
        sqt = [E(nc.sbuf_tensor(f"sqt{i}", [128, NT], BF16)) for i in range(2)]
        rs_g = E(nc.sbuf_tensor("rs_g", [128, HTN], F32))
        rs_u = E(nc.sbuf_tensor("rs_u", [128, HTN], F32))
        sg128 = E(nc.sbuf_tensor("sg128", [128, 1], F32))
        su128 = E(nc.sbuf_tensor("su128", [128, 1], F32))
        sd128 = E(nc.sbuf_tensor("sd128", [128, 1], F32))
        cu128 = E(nc.sbuf_tensor("cu128", [128, 1], F32))
        btq = E(nc.sbuf_tensor("btq", [128, 8], F32))
        bkv = E(nc.sbuf_tensor("bkv", [128, 16], F32))
        bqraw = E(nc.sbuf_tensor("bqraw", [128, 32], F32))

        pb = [E(nc.psum_tensor(f"pb{i}", [128, 512], F32)) for i in range(8)]
        sem = {s: E(nc.semaphore(s)) for s in SEMS}
        block = E(nc.Block())

        def ev(eng, fn):
            kb.ops[eng].append(fn)

        def wait(eng, s, v):
            if v > 0:
                kb.ops[eng].append(lambda e, s=s, v=v: e.wait_ge(sem[s], v))

        ev("sync", lambda e: e.dma_start(out=ident[:, :], in_=id_d.ap()).then_inc(sem["LC"], 16))
        ev("sync", lambda e: e.dma_start(out=bqraw[:, :], in_=bq_d.ap()).then_inc(sem["LC"], 16))
        ev("sync", lambda e: e.dma_start(out=bkv[:, 0:8], in_=bk_d.ap()).then_inc(sem["LC"], 16))
        ev("sync", lambda e: e.dma_start(out=bkv[:, 8:16], in_=bv_d.ap()).then_inc(sem["LC"], 16))
        ev("sync", lambda e: e.dma_start(out=sg128[:, :], in_=sg_d.ap().broadcast_to([128, 1])).then_inc(sem["LC"], 16))
        ev("sync", lambda e: e.dma_start(out=su128[:, :], in_=su_d.ap().broadcast_to([128, 1])).then_inc(sem["LC"], 16))
        ev("sync", lambda e: e.dma_start(out=sd128[:, :], in_=sd_d.ap().broadcast_to([128, 1])).then_inc(sem["LC"], 16))
        ly_consts = kb.bump("LC", 7 * 16)

        wait("vector", "LC", ly_consts)
        ev("vector", lambda e: e.memset(onesb[:, :], 1.0))
        ev("vector", lambda e: e.memset(ones1f[:, :], 1.0))
        ev("vector", lambda e: e.memset(epst[:, :], EPS))
        ev("vector", lambda e: e.tensor_mul(cu128[:, :], su128[:, :], sd128[:, :]).then_inc(sem["SV"], 1))
        v1 = kb.bump("SV")
        wait("vector", "SV", v1)
        ev("vector", lambda e: e.tensor_scalar_mul(cu128[:, :], cu128[:, :], 0.5).then_inc(sem["SV"], 1))
        kb.bump("SV")
        ev("vector", lambda e: e.reduce_sum(
            btq[:, :], bqraw[:, :].rearrange("p (g j) -> p g j", g=8),
            axis=AX.X).then_inc(sem["SV"], 1))
        kb.bump("SV")

        with ExitStack() as ph:
            xin_f = [ph.enter_context(nc.sbuf_tensor(f"xinf{i}", [128, 4096], F32)) for i in range(2)]
            xin_b = [ph.enter_context(nc.sbuf_tensor(f"xinb{i}", [128, 4096], BF16)) for i in range(2)]

            ld_m = [0] * 8
            cast_m = [0] * 8
            tp_m = [0] * 8
            bank_ev = {0: 0, 1: 0}
            for tp in range(8):
                b = tp % 2
                lsem = "LY%d" % (tp % 2)
                if tp >= 2:
                    wait("sync", "SS", cast_m[tp - 2])
                ev("sync", lambda e, tp=tp, b=b, lsem=lsem: e.dma_start(
                    out=xin_f[b][:, :], in_=x_d.ap()[tp * 128:(tp + 1) * 128, :]).then_inc(sem[lsem], 16))
                ld_m[tp] = kb.bump(lsem, 16)
                wait("scalar", lsem, ld_m[tp])
                if tp >= 2:
                    wait("scalar", "SP", tp_m[tp - 2])
                ev("scalar", lambda e, b=b: e.activation(
                    out=xin_b[b][:, :], in_=xin_f[b][:, :], func=AF.Copy).then_inc(sem["SS"], 1))
                cast_m[tp] = kb.bump("SS")
                wait("tensor", "SS", cast_m[tp])
                for a in range(4):
                    bank = a % 2
                    wait("tensor", "SV", bank_ev[bank])
                    for jj in range(8):
                        j = a * 8 + jj
                        last = (jj == 7)
                        def tf(e, b=b, j=j, bank=bank, jj=jj, last=last):
                            ins = e.transpose(
                                pb[bank][:, :].bitcast(BF16)[:, jj * 128:(jj + 1) * 128],
                                xin_b[b][:, j::32], ident[:, :])
                            if last:
                                ins.then_inc(sem["SP"], 1)
                        ev("tensor", tf)
                    gm = kb.bump("SP")
                    wait("vector", "SP", gm)
                    def ef(e, bank=bank, a=a, tp=tp):
                        dst = Xt[:, :].rearrange("p (j t) -> p j t", j=DPT)[
                            :, a * 8:(a + 1) * 8, tp * 128:(tp + 1) * 128]
                        e.tensor_copy(dst, pb[bank][:, :].bitcast(BF16)[:, 0:1024]
                                      .rearrange("p (j t) -> p j t", j=8)).then_inc(sem["SV"], 1)
                    ev("vector", ef)
                    bank_ev[bank] = kb.bump("SV")
                tp_m[tp] = kb.cnt["SP"]
        x_ready = kb.cnt["SV"]

        def emit_stats(x_rdy):
            wait("scalar", "SV", x_rdy)
            wait("tensor", "SV", x_rdy)
            sq_m = [0] * DPT
            s2_m = [0] * DPT
            for j in range(DPT):
                b = j % 2
                if j >= 2:
                    wait("scalar", "SP", s2_m[j - 2])
                ev("scalar", lambda e, j=j, b=b: e.activation(
                    out=sqt[b][:, :], in_=Xt[:, j * NT:(j + 1) * NT],
                    func=AF.Square).then_inc(sem["SS"], 1))
                sq_m[j] = kb.bump("SS")
                for ncc in range(2):
                    ev("tensor", lambda e, j=j, ncc=ncc: e.matmul(
                        pb[ncc][0:1, :], onesb[:, :],
                        Xt[:, j * NT + ncc * 512: j * NT + (ncc + 1) * 512],
                        start=(j == 0), stop=(j == DPT - 1)))
                wait("tensor", "SS", sq_m[j])
                for ncc in range(2):
                    last = (ncc == 1)
                    def f(e, j=j, b=b, ncc=ncc, last=last):
                        ins = e.matmul(pb[2 + ncc][0:1, :], onesb[:, :],
                                       sqt[b][:, ncc * 512:(ncc + 1) * 512],
                                       start=(j == 0), stop=(j == DPT - 1))
                        if last:
                            ins.then_inc(sem["SP"], 1)
                    ev("tensor", f)
                s2_m[j] = kb.bump("SP")
            mm_done = s2_m[DPT - 1]

            wait("vector", "SP", mm_done)
            ev("vector", lambda e: e.tensor_copy(s1row[0:1, 0:512], pb[0][0:1, :]))
            ev("vector", lambda e: e.tensor_copy(s1row[0:1, 512:1024], pb[1][0:1, :]))
            ev("vector", lambda e: e.tensor_copy(s2row[0:1, 0:512], pb[2][0:1, :]))
            ev("vector", lambda e: e.tensor_copy(s2row[0:1, 512:1024], pb[3][0:1, :]).then_inc(sem["SV"], 1))
            m1 = kb.bump("SV")
            wait("vector", "SV", m1)
            ev("vector", lambda e: e.tensor_scalar_mul(murow[0:1, :], s1row[0:1, :], 1.0 / D).then_inc(sem["SV"], 1))
            m2 = kb.bump("SV")
            wait("vector", "SV", m2)
            ev("vector", lambda e: e.tensor_mul(varow[0:1, :], murow[0:1, :], murow[0:1, :]).then_inc(sem["SV"], 1))
            m3 = kb.bump("SV")
            wait("vector", "SV", m3)
            ev("vector", lambda e: e.scalar_tensor_tensor(
                varow[0:1, :], s2row[0:1, :], 1.0 / D, varow[0:1, :],
                op0=ALU.mult, op1=ALU.subtract).then_inc(sem["SV"], 1))
            m4 = kb.bump("SV")
            wait("scalar", "SV", m4)
            ev("scalar", lambda e: e.activation(out=nrsrow[0:1, :], in_=varow[0:1, :],
                                                func=AF.Ln, bias=epst[0:1, :]).then_inc(sem["SS"], 1))
            m5 = kb.bump("SS")
            wait("scalar", "SS", m5)
            ev("scalar", lambda e: e.activation(out=nrsrow[0:1, :], in_=nrsrow[0:1, :],
                                                func=AF.Exp, scale=-0.5).then_inc(sem["SS"], 1))
            m6 = kb.bump("SS")
            wait("vector", "SS", m6)
            ev("vector", lambda e: e.tensor_scalar_mul(nrsrow[0:1, :], nrsrow[0:1, :], -1.0).then_inc(sem["SV"], 1))
            m7 = kb.bump("SV")
            wait("tensor", "SV", m7)
            for ncc in range(2):
                ev("tensor", lambda e, ncc=ncc: e.matmul(
                    pb[4 + ncc][:, :], ones1f[:, :], murow[0:1, ncc * 512:(ncc + 1) * 512],
                    start=True, stop=True))
                last = (ncc == 1)
                def f(e, ncc=ncc, last=last):
                    ins = e.matmul(pb[6 + ncc][:, :], ones1f[:, :],
                                   nrsrow[0:1, ncc * 512:(ncc + 1) * 512], start=True, stop=True)
                    if last:
                        ins.then_inc(sem["SP"], 1)
                ev("tensor", f)
            m8 = kb.bump("SP")
            wait("vector", "SP", m8)
            ev("vector", lambda e: e.tensor_copy(mub[:, 0:512], pb[4][:, :]))
            ev("vector", lambda e: e.tensor_copy(mub[:, 512:1024], pb[5][:, :]))
            ev("vector", lambda e: e.tensor_copy(nrstdb[:, 0:512], pb[6][:, :]))
            ev("vector", lambda e: e.tensor_copy(nrstdb[:, 512:1024], pb[7][:, :]).then_inc(sem["SV"], 1))
            return kb.bump("SV")

        def emit_ffn(pass_idx, x_rdy, force_first=False):
            stats_mark = emit_stats(x_rdy)
            first_pass = (pass_idx == 1) or force_first
            wait("gpsimd", "SV", x_rdy)
            wait("sync", "SV", x_rdy)

            with ExitStack() as ph:
                PE = ph.enter_context
                wrow_g = [PE(nc.sbuf_tensor(f"p{pass_idx}wg{i}", [128, 4096], BF16)) for i in range(2)]
                wrow_u = [PE(nc.sbuf_tensor(f"p{pass_idx}wu{i}", [128, 4096], BF16)) for i in range(2)]
                lhs_g = [PE(nc.sbuf_tensor(f"p{pass_idx}lg{i}", [128, 4096], BF16)) for i in range(2)]
                lhs_u = [PE(nc.sbuf_tensor(f"p{pass_idx}lu{i}", [128, 4096], BF16)) for i in range(2)]
                w00 = PE(nc.sbuf_tensor(f"p{pass_idx}w00", [128, 512], F32))
                w10 = PE(nc.sbuf_tensor(f"p{pass_idx}w10", [128, 512], F32))
                w01 = PE(nc.sbuf_tensor(f"p{pass_idx}w01", [128, 512], F32))
                w11 = PE(nc.sbuf_tensor(f"p{pass_idx}w11", [128, 512], F32))
                sl0 = PE(nc.sbuf_tensor(f"p{pass_idx}sl0", [128, 512], F32))
                sl1 = PE(nc.sbuf_tensor(f"p{pass_idx}sl1", [128, 512], F32))
                mst = [PE(nc.sbuf_tensor(f"p{pass_idx}ms{i}", [128, 1024], BF16)) for i in range(2)]

                ld_m = [0] * (HTN + 2)
                ldy_m = [0] * (HTN + 2)
                tp_done = [0] * (HTN + 1)
                evac_done = [0] * (HTN + 1)
                rs_done = [0] * (HTN + 1)
                wb_done = [0] * (HTN + 1)
                c1_done = [0] * (HTN + 1)
                mid_done = [0] * (HTN + 1)
                silu_done = [0] * (HTN + 1)
                mm_done = [0] * (HTN + 1)
                midwb = [0] * (HTN + 2)
                evgrp = [stats_mark, stats_mark]

                def emit_load(m):
                    b = m % 2
                    if first_pass:
                        lsem = "LG%d" % (m % 2)
                        if m >= 2:
                            wait("gpsimd", "SP", tp_done[m - 2])
                            wait("gpsimd", "SV", rs_done[m - 2])
                        ev("gpsimd", lambda e, m=m, b=b, lsem=lsem: e.dma_start(
                            out=wrow_g[b][:, :], in_=wg_d.ap()[m * 128:(m + 1) * 128, :]).then_inc(sem[lsem], 16))
                        kb.bump(lsem, 16)
                        ev("gpsimd", lambda e, m=m, b=b, lsem=lsem: e.dma_start(
                            out=wrow_u[b][:, :], in_=wu_d.ap()[m * 128:(m + 1) * 128, :]).then_inc(sem[lsem], 16))
                        ld_m[m] = kb.bump(lsem, 16)
                    else:
                        lsem = "LY%d" % (m % 2)
                        if m >= 2:
                            wait("sync", "SP", mm_done[m - 2])
                        ev("sync", lambda e, m=m, b=b, lsem=lsem: e.dma_start(
                            out=lhs_g[b][:, :], in_=wgT_s.ap()[m]).then_inc(sem[lsem], 16))
                        kb.bump(lsem, 16)
                        ev("sync", lambda e, m=m, b=b, lsem=lsem: e.dma_start(
                            out=lhs_u[b][:, :], in_=wuT_s.ap()[m]).then_inc(sem[lsem], 16))
                        ldy_m[m] = kb.bump(lsem, 16)

                def emit_transposes(m):
                    b = m % 2
                    wait("vector", "LG%d" % (m % 2), ld_m[m])
                    ev("vector", lambda e, m=m, b=b: e.reduce_sum(
                        rs_g[:, m:m + 1], wrow_g[b][:, :], axis=AX.X))
                    ev("vector", lambda e, m=m, b=b: e.reduce_sum(
                        rs_u[:, m:m + 1], wrow_u[b][:, :], axis=AX.X).then_inc(sem["SV"], 1))
                    rs_done[m] = kb.bump("SV")
                    if m >= 2:
                        wait("vector", "WB%d" % (m % 2), wb_done[m - 2])
                    wait("tensor", "LG%d" % (m % 2), ld_m[m])
                    for a in range(8):
                        bank = 4 + (a % 2)
                        src = wrow_g[b] if a < 4 else wrow_u[b]
                        joff = (a % 4) * 8
                        wait("tensor", "SV", evgrp[bank - 4])
                        for jj in range(8):
                            j = joff + jj
                            last = (jj == 7)
                            def tf(e, src=src, j=j, bank=bank, jj=jj, last=last):
                                ins = e.transpose(
                                    pb[bank][:, :].bitcast(BF16)[:, jj * 128:(jj + 1) * 128],
                                    src[:, j::32], ident[:, :])
                                if last:
                                    ins.then_inc(sem["SP"], 1)
                            ev("tensor", tf)
                        gm = kb.bump("SP")
                        dstbuf = lhs_g[b] if a < 4 else lhs_u[b]
                        wait("vector", "SP", gm)
                        def ef(e, dstbuf=dstbuf, bank=bank, a=a):
                            e.tensor_copy(dstbuf[:, (a % 4) * 1024:((a % 4) + 1) * 1024],
                                          pb[bank][:, :].bitcast(BF16)[:, 0:1024]).then_inc(sem["SV"], 1)
                        ev("vector", ef)
                        evgrp[bank - 4] = kb.bump("SV")
                    tp_done[m] = kb.cnt["SP"]
                    evac_done[m] = kb.cnt["SV"]
                    wait("sync", "SV", evac_done[m])
                    wsem = "WB%d" % (m % 2)
                    ev("sync", lambda e, m=m, b=b, wsem=wsem: e.dma_start(
                        out=wgT_s.ap()[m], in_=lhs_g[b][:, :]).then_inc(sem[wsem], 16))
                    kb.bump(wsem, 16)
                    ev("sync", lambda e, m=m, b=b, wsem=wsem: e.dma_start(
                        out=wuT_s.ap()[m], in_=lhs_u[b][:, :]).then_inc(sem[wsem], 16))
                    wb_done[m] = kb.bump(wsem, 16)

                def emit_mms(m):
                    b = m % 2
                    if first_pass:
                        wait("tensor", "SV", evac_done[m])
                    else:
                        wait("tensor", "LY%d" % (m % 2), ldy_m[m])
                        wait("tensor", "SV", stats_mark)
                    if m >= 1:
                        wait("tensor", "SV", c1_done[m - 1])
                    for kt in range(32):
                        for (lbuf, bank0) in ((lhs_g[b], 0), (lhs_u[b], 2)):
                            for ncc in range(2):
                                last = (kt == 31 and bank0 == 2 and ncc == 1)
                                def mf(e, lbuf=lbuf, kt=kt, bank0=bank0, ncc=ncc, last=last):
                                    ins = e.matmul(
                                        pb[bank0 + ncc][:, :],
                                        lbuf[:, kt * 128:(kt + 1) * 128],
                                        Xt[:, kt * NT + ncc * 512: kt * NT + (ncc + 1) * 512],
                                        start=(kt == 0), stop=(kt == 31))
                                    if last:
                                        ins.then_inc(sem["SP"], 1)
                                ev("tensor", mf)
                    mm_done[m] = kb.bump("SP")

                def emit_chain(m):
                    b = m % 2
                    wait("vector", "SP", mm_done[m])
                    if m >= 1:
                        wait("vector", "SS", silu_done[m - 1])
                    ev("vector", lambda e, m=m: e.scalar_tensor_tensor(
                        w00[:, :], mub[:, 0:512], rs_g[:, m:m + 1], pb[0][:, :],
                        op0=ALU.mult, op1=ALU.subtract))
                    ev("vector", lambda e, m=m: e.scalar_tensor_tensor(
                        w10[:, :], mub[:, 0:512], rs_u[:, m:m + 1], pb[2][:, :],
                        op0=ALU.mult, op1=ALU.subtract))
                    ev("vector", lambda e, m=m: e.scalar_tensor_tensor(
                        w01[:, :], mub[:, 512:1024], rs_g[:, m:m + 1], pb[1][:, :],
                        op0=ALU.mult, op1=ALU.subtract))
                    ev("vector", lambda e, m=m: e.scalar_tensor_tensor(
                        w11[:, :], mub[:, 512:1024], rs_u[:, m:m + 1], pb[3][:, :],
                        op0=ALU.mult, op1=ALU.subtract).then_inc(sem["SV"], 1))
                    c1_done[m] = kb.bump("SV")
                    wait("vector", "SV", c1_done[m])
                    ev("vector", lambda e: e.tensor_mul(w00[:, :], w00[:, :], nrstdb[:, 0:512]))
                    ev("vector", lambda e: e.scalar_tensor_tensor(
                        w10[:, :], w10[:, :], cu128[:, :], nrstdb[:, 0:512],
                        op0=ALU.mult, op1=ALU.mult))
                    ev("vector", lambda e: e.tensor_mul(w01[:, :], w01[:, :], nrstdb[:, 512:1024]))
                    ev("vector", lambda e: e.scalar_tensor_tensor(
                        w11[:, :], w11[:, :], cu128[:, :], nrstdb[:, 512:1024],
                        op0=ALU.mult, op1=ALU.mult).then_inc(sem["SV"], 1))
                    c2m = kb.bump("SV")
                    wait("scalar", "SV", c2m)
                    if m >= 2:
                        wait("scalar", "WM%d" % (m % 2), midwb[m - 2])
                    ev("scalar", lambda e: e.activation(out=sl0[:, :], in_=w00[:, :],
                                                        func=AF.Silu, scale=sg128[:, :]))
                    ev("scalar", lambda e: e.activation(out=sl1[:, :], in_=w01[:, :],
                                                        func=AF.Silu, scale=sg128[:, :]).then_inc(sem["SS"], 1))
                    silu_done[m] = kb.bump("SS")
                    wait("vector", "SS", silu_done[m])
                    if m >= 2:
                        wait("vector", "WM%d" % (m % 2), midwb[m - 2])
                    ev("vector", lambda e, b=b: e.tensor_mul(mst[b][:, 0:512], sl0[:, :], w10[:, :]))
                    ev("vector", lambda e, b=b: e.tensor_mul(mst[b][:, 512:1024], sl1[:, :], w11[:, :]).then_inc(sem["SV"], 1))
                    mid_done[m] = kb.bump("SV")
                    wait("sync", "SV", mid_done[m])
                    msem = "WM%d" % (m % 2)
                    ev("sync", lambda e, m=m, b=b, msem=msem: e.dma_start(
                        out=mid_s.ap()[m], in_=mst[b][:, :]).then_inc(sem[msem], 16))
                    midwb[m] = kb.bump(msem, 16)

                if first_pass:
                    emit_load(0)
                    emit_load(1)
                    emit_transposes(0)
                    for m in range(HTN):
                        if m + 2 < HTN:
                            emit_load(m + 2)
                        if m + 1 < HTN:
                            emit_transposes(m + 1)
                        emit_mms(m)
                        emit_chain(m)
                else:
                    emit_load(0)
                    emit_load(1)
                    for m in range(HTN):
                        emit_mms(m)
                        if m + 2 < HTN:
                            emit_load(m + 2)
                        emit_chain(m)


            with ExitStack() as ph:
                PE = ph.enter_context
                midin = PE(nc.sbuf_tensor(f"p{pass_idx}mi", [128, HTN * 512], BF16))
                wrow_d = [PE(nc.sbuf_tensor(f"p{pass_idx}wd{i}", [128, WCH * 128], BF16)) for i in range(2)]
                lhs_d = [PE(nc.sbuf_tensor(f"p{pass_idx}ldn{i}", [128, WCH * 128], BF16)) for i in range(2)]

                nwch = (HTN + WCH - 1) // WCH
                wchs = [min(WCH, HTN - i * WCH) for i in range(nwch)]
                seq = [(dpt, c) for dpt in range(DPT) for c in range(nwch)]

                dwb0 = {}
                dn_prev = [stats_mark, stats_mark]
                for ncc in range(2):
                    build_T = first_pass and ncc == 0
                    wait("sync", "SP", kb.cnt["SP"])
                    def mf(e, ncc=ncc):
                        src = mid_s.ap()[:, :, ncc * 512:(ncc + 1) * 512].transpose([1, 0, 2])
                        e.dma_start(out=midin[:, :].rearrange("p (k t) -> p k t", k=HTN),
                                    in_=src).then_inc(sem["MI"], 16)
                    ev("sync", mf)
                    mid_ld = kb.bump("MI", 16)

                    dl_mark = {}
                    dl_sem = {}
                    dl_evac = {}
                    dtp_done = {}
                    dch_done = {}
                    dn_upd = [0] * (DPT + 1)
                    devgrp = [kb.cnt["SV"], kb.cnt["SV"]]
                    ncc_entry_sp = kb.cnt["SP"]

                    def emit_dload(i):
                        dpt, c = seq[i]
                        b = i % 2
                        nk = wchs[c]
                        if build_T:
                            lsem = "LG%d" % (i % 2)
                            if i >= 2:
                                wait("gpsimd", "SP", dtp_done.get(seq[i - 2], 0))
                            else:
                                wait("gpsimd", "SP", ncc_entry_sp)
                            def f(e, dpt=dpt, c=c, b=b, nk=nk, lsem=lsem):
                                src = wd_d.ap().rearrange("(r j) h -> j r h", j=32)[dpt][
                                    :, c * WCH * 128: c * WCH * 128 + nk * 128]
                                e.dma_start(out=wrow_d[b][:, 0:nk * 128], in_=src).then_inc(sem[lsem], 16)
                            ev("gpsimd", f)
                            dl_mark[(dpt, c)] = kb.bump(lsem, 16)
                        else:
                            lsem = "LY%d" % (i % 2)
                            if i >= 2:
                                wait("sync", "SP", dch_done.get(seq[i - 2], 0))
                            else:
                                wait("sync", "SP", ncc_entry_sp)
                            def f(e, dpt=dpt, c=c, b=b, nk=nk, lsem=lsem):
                                src = wdT_s.ap()[dpt, c * WCH:c * WCH + nk].transpose([1, 0, 2])
                                e.dma_start(out=lhs_d[b][:, 0:nk * 128].rearrange("p (k f) -> p k f", k=nk),
                                            in_=src).then_inc(sem[lsem], 16)
                            ev("sync", f)
                            dl_mark[(dpt, c)] = kb.bump(lsem, 16)
                        dl_sem[(dpt, c)] = lsem

                    def emit_dtrans(i):
                        dpt, c = seq[i]
                        b = i % 2
                        nk = wchs[c]
                        wait("tensor", dl_sem[(dpt, c)], dl_mark[(dpt, c)])
                        if i >= 2:
                            wait("vector", "DW%d" % (i % 2), dwb0.get(seq[i - 2], 0))
                        ngr = (nk + 7) // 8
                        for a in range(ngr):
                            cnt = min(8, nk - a * 8)
                            bank = 4 + (a % 2)
                            wait("tensor", "SV", devgrp[bank - 4])
                            for jj in range(cnt):
                                kt = a * 8 + jj
                                last = (jj == cnt - 1)
                                def tf(e, b=b, kt=kt, bank=bank, jj=jj, last=last):
                                    ins = e.transpose(
                                        pb[bank][:, :].bitcast(BF16)[:, jj * 128:(jj + 1) * 128],
                                        wrow_d[b][:, kt * 128:(kt + 1) * 128], ident[:, :])
                                    if last:
                                        ins.then_inc(sem["SP"], 1)
                                ev("tensor", tf)
                            gm = kb.bump("SP")
                            wait("vector", "SP", gm)
                            def ef(e, b=b, bank=bank, a=a, cnt=cnt):
                                e.tensor_copy(lhs_d[b][:, a * 1024:a * 1024 + cnt * 128],
                                              pb[bank][:, :].bitcast(BF16)[:, 0:cnt * 128]).then_inc(sem["SV"], 1)
                            ev("vector", ef)
                            devgrp[bank - 4] = kb.bump("SV")
                        dtp_done[(dpt, c)] = kb.cnt["SP"]
                        dl_evac[(dpt, c)] = kb.cnt["SV"]
                        wait("sync", "SV", dl_evac[(dpt, c)])
                        dsem = "DW%d" % (i % 2)
                        def wf(e, dpt=dpt, c=c, b=b, nk=nk, dsem=dsem):
                            dst = wdT_s.ap()[dpt, c * WCH:c * WCH + nk].transpose([1, 0, 2])
                            e.dma_start(out=dst, in_=lhs_d[b][:, 0:nk * 128]
                                        .rearrange("p (k f) -> p k f", k=nk)).then_inc(sem[dsem], 16)
                        ev("sync", wf)
                        dwb0[(dpt, c)] = kb.bump(dsem, 16)

                    wait("tensor", "MI", mid_ld)
                    emit_dload(0)
                    for i, (dpt, c) in enumerate(seq):
                        b = i % 2
                        nk = wchs[c]
                        if i + 1 < len(seq):
                            emit_dload(i + 1)
                        if build_T:
                            emit_dtrans(i)
                            wait("tensor", "SV", dl_evac[(dpt, c)])
                        else:
                            wait("tensor", dl_sem[(dpt, c)], dl_mark[(dpt, c)])
                        bank = 6 + (dpt % 2)
                        if c == 0:
                            if dpt >= 2:
                                wait("tensor", "SV", dn_upd[dpt - 2])
                            else:
                                wait("tensor", "SV", dn_prev[dpt])
                        for kt in range(nk):
                            gk = c * WCH + kt
                            last = (kt == nk - 1)
                            def mmf(e, b=b, kt=kt, gk=gk, bank=bank, last=last):
                                ins = e.matmul(
                                    pb[bank][:, :],
                                    lhs_d[b][:, kt * 128:(kt + 1) * 128],
                                    midin[:, gk * 512:(gk + 1) * 512],
                                    start=(gk == 0), stop=(gk == HTN - 1))
                                if last:
                                    ins.then_inc(sem["SP"], 1)
                            ev("tensor", mmf)
                        dch_done[(dpt, c)] = kb.bump("SP")
                        if c == nwch - 1:
                            wait("vector", "SP", dch_done[(dpt, c)])
                            def uf(e, dpt=dpt, bank=bank, ncc=ncc):
                                dst = Xt[:, dpt * NT + ncc * 512: dpt * NT + (ncc + 1) * 512]
                                e.tensor_add(dst, dst, pb[bank][:, :]).then_inc(sem["SV"], 1)
                            ev("vector", uf)
                            dn_upd[dpt] = kb.bump("SV")
                    dn_prev = [dn_upd[DPT - 2], dn_upd[DPT - 1]]
            return kb.cnt["SV"]

        def emit_attn(x_rdy):
            stats_mark = emit_stats(x_rdy)
            wait("gpsimd", "SV", x_rdy)
            wait("sync", "SV", x_rdy)

            with ExitStack() as ph:
                PE = ph.enter_context
                Ysub = PE(nc.sbuf_tensor("at_y", [128, DPT * 512], BF16))
                ytmp = [PE(nc.sbuf_tensor(f"at_yt{i}", [128, 512], F32)) for i in range(2)]
                qraw = [PE(nc.sbuf_tensor(f"at_qr{i}", [128, 4096], BF16)) for i in range(2)]
                kraw = [PE(nc.sbuf_tensor(f"at_kr{i}", [128, 1024], BF16)) for i in range(2)]
                vraw = [PE(nc.sbuf_tensor(f"at_vr{i}", [128, 1024], BF16)) for i in range(2)]
                wtq = [PE(nc.sbuf_tensor(f"at_wq{i}", [128, 1024], BF16)) for i in range(2)]
                Qph = PE(nc.sbuf_tensor("at_Q", [128, 512 * 8], BF16))
                Kph = PE(nc.sbuf_tensor("at_K", [128, 512 * 8], BF16))
                Vph = PE(nc.sbuf_tensor("at_V", [128, 512 * 9], BF16))
                qst = PE(nc.sbuf_tensor("at_qs", [128, 96], BF16))
                kst = PE(nc.sbuf_tensor("at_ks", [128, 96], BF16))
                qTs = PE(nc.sbuf_tensor("at_qT", [128, 128], BF16))
                kTs = PE(nc.sbuf_tensor("at_kT", [128, 128], BF16))
                m3 = PE(nc.sbuf_tensor("at_m3", [128, 3], F32))
                Esb = PE(nc.sbuf_tensor("at_E", [128, 384], BF16))
                ETs = PE(nc.sbuf_tensor("at_ET", [128, 384], BF16))
                Oph = PE(nc.sbuf_tensor("at_O", [128, 8 * 512], BF16))
                Rde = PE(nc.sbuf_tensor("at_R", [128, 512], F32))
                OR2 = PE(nc.sbuf_tensor("at_OR", [128, 512], F32))

                for half in range(2):
                    hoff = half * 512
                    wait("vector", "SV", stats_mark)
                    for j in range(DPT):
                        yb = j % 2
                        def yf1(e, j=j, hoff=hoff, yb=yb):
                            e.tensor_sub(ytmp[yb][:, :], mub[:, hoff:hoff + 512],
                                         Xt[:, j * NT + hoff:j * NT + hoff + 512]).then_inc(sem["SV"], 1)
                        ev("vector", yf1)
                        t1 = kb.bump("SV")
                        wait("vector", "SV", t1)
                        def yf2(e, j=j, hoff=hoff, yb=yb):
                            e.tensor_mul(Ysub[:, j * 512:(j + 1) * 512], ytmp[yb][:, :],
                                         nrstdb[:, hoff:hoff + 512]).then_inc(sem["SV"], 1)
                        ev("vector", yf2)
                        kb.bump("SV")
                    y_ready = kb.cnt["SV"]

                    ev("vector", lambda e: e.memset(
                        Vph[:, :].rearrange("p (t c) -> p t c", t=512)[:, :, 8:9], 1.0).then_inc(sem["SV"], 1))
                    ones_mark = kb.bump("SV")

                    sub_ev_done = kb.cnt["SV"]
                    for sub, (wsrc_d, rbuf, wid, dstp) in enumerate((
                            (wq_d, qraw, 4096, Qph),
                            (wk_d, kraw, 1024, Kph),
                            (wv_d, vraw, 1024, Vph))):
                        entry_sp = kb.cnt["SP"]
                        ld_marks = {}
                        kt_mm = {}
                        wt_marks = {}
                        nh = wid // 128

                        def emit_qkv_load(kt):
                            b = kt % 2
                            lsem = "LG%d" % (kt % 2)
                            if kt >= 2:
                                wait("gpsimd", "SP", kt_mm[kt - 2])
                            else:
                                wait("gpsimd", "SP", entry_sp)
                            def lf(e, kt=kt, b=b, wsrc_d=wsrc_d, rbuf=rbuf, wid=wid, nh=nh, lsem=lsem):
                                src = wsrc_d.ap().rearrange("(r j) d h -> j r d h", j=32)[kt]
                                dst = rbuf[b][:, 0:wid].rearrange("p (d h) -> p d h", h=nh)
                                e.dma_start(out=dst, in_=src).then_inc(sem[lsem], 16)
                            ev("gpsimd", lf)
                            ld_marks[kt] = kb.bump(lsem, 16)

                        emit_qkv_load(0)
                        emit_qkv_load(1)
                        for kt in range(32):
                            b = kt % 2
                            if sub == 0:
                                wait("vector", "LG%d" % (kt % 2), ld_marks[kt])
                                if kt >= 2:
                                    wait("vector", "SP", kt_mm[kt - 2])
                                def bf(e, b=b):
                                    src = qraw[b][:, :].rearrange("p (d g j) -> p d g j", g=8, j=4) \
                                        .transpose([0, 2, 1, 3])
                                    with nc.allow_low_precision(reason="bf16 store of fp32-internal 4-term weight group sum"):
                                        e.reduce_sum(
                                            wtq[b][:, :].rearrange("p (g d) -> p g d", g=8),
                                            src, axis=AX.X).then_inc(sem["SV"], 1)
                                ev("vector", bf)
                                wt_marks[kt] = kb.bump("SV")
                                wait("tensor", "SV", wt_marks[kt])
                                lsrc = wtq[b]
                            else:
                                wait("tensor", "LG%d" % (kt % 2), ld_marks[kt])
                                lsrc = rbuf[b]
                            if kt == 0:
                                wait("tensor", "SV", y_ready)
                                wait("tensor", "SV", sub_ev_done)
                            for g in range(8):
                                if sub == 0:
                                    lap = lsrc[:, g * 128:(g + 1) * 128]
                                else:
                                    lap = lsrc[:, g:wid:8]
                                last = (g == 7)
                                def mfq(e, lap=lap, g=g, kt=kt, last=last):
                                    ins = e.matmul(pb[g][:, :], lap,
                                                   Ysub[:, kt * 512:(kt + 1) * 512],
                                                   start=(kt == 0), stop=(kt == 31))
                                    if last:
                                        ins.then_inc(sem["SP"], 1)
                                ev("tensor", mfq)
                            kt_mm[kt] = kb.bump("SP")
                            if kt + 2 < 32:
                                emit_qkv_load(kt + 2)
                        wait("vector", "SP", kt_mm[31])
                        if sub == 2:
                            wait("vector", "SV", ones_mark)
                        for g in range(8):
                            last = (g == 7)
                            def ef(e, sub=sub, g=g, dstp=dstp, last=last):
                                nbl = 9 if sub == 2 else 8
                                dst = dstp[:, :].rearrange("p (t c) -> p t c", t=512)[:, :, g:g + 1].squeeze(2)
                                if sub == 0:
                                    ins = e.tensor_scalar(dst, pb[g][:, :], btq[:, g:g + 1], RS128,
                                                          op0=ALU.add, op1=ALU.mult)
                                elif sub == 1:
                                    ins = e.tensor_scalar_add(dst, pb[g][:, :], bkv[:, g:g + 1])
                                else:
                                    ins = e.tensor_scalar_add(dst, pb[g][:, :], bkv[:, 8 + g:9 + g])
                                if last:
                                    ins.then_inc(sem["SV"], 1)
                            ev("vector", ef)
                        sub_ev_done = kb.bump("SV")
                    qkv_done = sub_ev_done

                    ngroups = (512 + 2) // 3
                    mk = {"stage": 0, "tev": 0, "red": 0, "etev": 0, "ocp": 0,
                          "exp": kb.cnt["SS"], "tp": 0, "sc": 0, "et": 0, "om": 0}
                    wait("vector", "SV", qkv_done)
                    wait("tensor", "SV", qkv_done)
                    for gi in range(ngroups):
                        t0 = gi * 3
                        ng = min(3, 512 - t0)
                        if gi > 0:
                            wait("vector", "SP", mk["tp"])
                        def sf(e, t0=t0, ng=ng):
                            e.tensor_copy(
                                qst[:, 0:ng * 32].rearrange("p (t c) -> p t c", t=ng)[:, :, 0:8],
                                Qph[:, t0 * 8:(t0 + ng) * 8].rearrange("p (t c) -> p t c", t=ng))
                            e.tensor_copy(
                                kst[:, 0:ng * 32].rearrange("p (t c) -> p t c", t=ng)[:, :, 0:8],
                                Kph[:, t0 * 8:(t0 + ng) * 8].rearrange("p (t c) -> p t c", t=ng)).then_inc(sem["SV"], 1)
                        ev("vector", sf)
                        mk["stage"] = kb.bump("SV")
                        wait("tensor", "SV", mk["stage"])
                        if gi > 0:
                            wait("tensor", "SV", mk["tev"])
                        def tf(e, ng=ng):
                            e.transpose(pb[0][0:32 * ng, 0:64].bitcast(BF16), qst[:, 0:ng * 32], ident[:, :])
                            e.transpose(pb[1][0:32 * ng, 0:64].bitcast(BF16), kst[:, 0:ng * 32],
                                        ident[:, :]).then_inc(sem["SP"], 1)
                        ev("tensor", tf)
                        mk["tp"] = kb.bump("SP")
                        wait("vector", "SP", mk["tp"])
                        def vf(e, ng=ng):
                            e.tensor_copy(qTs[0:32 * ng, :], pb[0][0:32 * ng, 0:64].bitcast(BF16))
                            e.tensor_copy(kTs[0:32 * ng, :], pb[1][0:32 * ng, 0:64].bitcast(BF16)).then_inc(sem["SV"], 1)
                        ev("vector", vf)
                        mk["tev"] = kb.bump("SV")
                        wait("tensor", "SV", mk["tev"])
                        wait("tensor", "SS", mk["exp"])
                        if gi > 0:
                            wait("tensor", "SV", mk["etev"])
                        for i in range(ng):
                            last = (i == ng - 1)
                            def scf(e, i=i, last=last):
                                ins = e.matmul(pb[2 + i][:, 0:128],
                                               qTs[32 * i:32 * i + 8, :], kTs[32 * i:32 * i + 8, :],
                                               start=True, stop=True, tile_position=(32 * i, 0))
                                if last:
                                    ins.then_inc(sem["SP"], 1)
                            ev("tensor", scf)
                        mk["sc"] = kb.bump("SP")
                        wait("vector", "SP", mk["sc"])
                        wait("vector", "SS", mk["exp"])
                        for i in range(ng):
                            last = (i == ng - 1)
                            def rf(e, i=i, last=last):
                                ins = e.reduce_max(m3[:, i:i + 1], pb[2 + i][:, 0:128],
                                                   axis=AX.X, negate=True)
                                if last:
                                    ins.then_inc(sem["SV"], 1)
                            ev("vector", rf)
                        mk["red"] = kb.bump("SV")
                        wait("scalar", "SV", mk["red"])
                        if gi > 0:
                            wait("scalar", "SP", mk["et"])
                        for i in range(ng):
                            last = (i == ng - 1)
                            def xf(e, i=i, last=last):
                                ins = e.activation(out=Esb[:, i * 128:(i + 1) * 128],
                                                   in_=pb[2 + i][:, 0:128], func=AF.Exp,
                                                   bias=m3[:, i:i + 1])
                                if last:
                                    ins.then_inc(sem["SS"], 1)
                            ev("scalar", xf)
                        mk["exp"] = kb.bump("SS")
                        wait("tensor", "SS", mk["exp"])
                        for i in range(ng):
                            last = (i == ng - 1)
                            def etf(e, i=i, last=last):
                                ins = e.transpose(pb[2 + i][:, 0:64].bitcast(BF16),
                                                  Esb[:, i * 128:(i + 1) * 128], ident[:, :])
                                if last:
                                    ins.then_inc(sem["SP"], 1)
                            ev("tensor", etf)
                        mk["et"] = kb.bump("SP")
                        wait("vector", "SP", mk["et"])
                        for i in range(ng):
                            last = (i == ng - 1)
                            def eef(e, i=i, last=last):
                                ins = e.tensor_copy(ETs[:, i * 128:(i + 1) * 128],
                                                    pb[2 + i][:, 0:64].bitcast(BF16))
                                if last:
                                    ins.then_inc(sem["SV"], 1)
                            ev("vector", eef)
                        mk["etev"] = kb.bump("SV")
                        wait("tensor", "SV", mk["etev"])
                        if gi > 0:
                            wait("tensor", "SV", mk["ocp"])
                        for i in range(ng):
                            t = t0 + i
                            last = (i == ng - 1)
                            def omf(e, i=i, t=t, last=last):
                                ins = e.matmul(pb[5][:, 9 * i:9 * i + 9],
                                               ETs[:, i * 128:(i + 1) * 128],
                                               Vph[:, 9 * t:9 * t + 9],
                                               start=True, stop=True)
                                if last:
                                    ins.then_inc(sem["SP"], 1)
                            ev("tensor", omf)
                        mk["om"] = kb.bump("SP")
                        wait("vector", "SP", mk["om"])
                        def cf(e, t0=t0, ng=ng):
                            src = pb[5][:, 0:9 * ng].rearrange("p (i c) -> p i c", i=ng)
                            dstO = Oph[:, :].rearrange("p (g t) -> p g t", g=8)[:, :, t0:t0 + ng] \
                                .transpose([0, 2, 1])
                            e.tensor_copy(dstO, src[:, :, 0:8])
                            e.tensor_copy(Rde[:, t0:t0 + ng], src[:, :, 8:9].squeeze(2)).then_inc(sem["SV"], 1)
                        ev("vector", cf)
                        mk["ocp"] = kb.bump("SV")

                    wait("vector", "SV", mk["ocp"])
                    ev("vector", lambda e: e.reciprocal(Rde[:, :], Rde[:, :]).then_inc(sem["SV"], 1))
                    rr_m = kb.bump("SV")
                    wait("vector", "SV", rr_m)
                    for g in range(8):
                        def of(e, g=g):
                            e.tensor_mul(OR2[:, :], Oph[:, g * 512:(g + 1) * 512], Rde[:, :]).then_inc(sem["SV"], 1)
                        ev("vector", of)
                        orm = kb.bump("SV")
                        wait("vector", "SV", orm)
                        for j in range(4):
                            pt = 4 * g + j
                            last = (j == 3)
                            def af(e, pt=pt, hoff=hoff, last=last):
                                dst = Xt[:, pt * NT + hoff: pt * NT + hoff + 512]
                                ins = e.tensor_add(dst, dst, OR2[:, :])
                                if last:
                                    ins.then_inc(sem["SV"], 1)
                            ev("vector", af)
                        kb.bump("SV")
            return kb.cnt["SV"]

        def emit_output(x_rdy):
            with ExitStack() as ph:
                xout = [ph.enter_context(nc.sbuf_tensor(f"xo{i}", [128, 4096], F32)) for i in range(2)]
                wait("tensor", "SV", x_rdy)
                st_m = [0] * 8
                bank_ev = {0: kb.cnt["SV"], 1: kb.cnt["SV"]}
                for tp in range(8):
                    b = tp % 2
                    for a in range(4):
                        bank = a % 2
                        wait("tensor", "SV", bank_ev[bank])
                        for jj in range(8):
                            j = a * 8 + jj
                            last = (jj == 7)
                            def tf(e, j=j, tp=tp, bank=bank, jj=jj, last=last):
                                ins = e.transpose(
                                    pb[bank][:, :].bitcast(BF16)[:, jj * 128:(jj + 1) * 128],
                                    Xt[:, j * NT + tp * 128: j * NT + (tp + 1) * 128], ident[:, :])
                                if last:
                                    ins.then_inc(sem["SP"], 1)
                            ev("tensor", tf)
                        gm = kb.bump("SP")
                        wait("vector", "SP", gm)
                        if a == 0 and tp >= 2:
                            wait("vector", "SO%d" % (tp % 2), st_m[tp - 2])
                        def ef(e, bank=bank, a=a, b=b):
                            src = pb[bank][:, :].bitcast(BF16)[:, 0:1024].rearrange("p (j t) -> p j t", j=8)
                            dst = xout[b][:, :].rearrange("p (t j) -> p t j", j=32)[:, :, a * 8:(a + 1) * 8] \
                                .transpose([0, 2, 1])
                            e.tensor_copy(dst, src).then_inc(sem["SV"], 1)
                        ev("vector", ef)
                        bank_ev[bank] = kb.bump("SV")
                    evdone = kb.cnt["SV"]
                    wait("sync", "SV", evdone)
                    osem = "SO%d" % (tp % 2)
                    ev("sync", lambda e, tp=tp, b=b, osem=osem: e.dma_start(
                        out=out_d.ap()[tp * 128:(tp + 1) * 128, :], in_=xout[b][:, :]).then_inc(sem[osem], 16))
                    st_m[tp] = kb.bump(osem, 16)
                if dbg2_d is not None:
                    wait("gpsimd", "SV", x_rdy)
                    ev("gpsimd", lambda e: e.dma_start(
                        out=dbg2_d.ap()[:, 2560:3584], in_=mid_s.ap()[0]).then_inc(sem["SO0"], 16))
                    kb.bump("SO0", 16)
                    ev("gpsimd", lambda e: e.dma_start(
                        out=dbg2_d.ap()[:, 3584:4096], in_=wgT_s.ap()[0][:, 0:512]).then_inc(sem["SO0"], 16))
                    kb.bump("SO0", 16)
                    for dst_sl, src_ap in (
                            ((slice(0, 128), slice(0, 1024)), mub[:, :]),
                            ((slice(0, 128), slice(1024, 2048)), nrstdb[:, :]),
                            ((slice(0, 128), slice(2048, 2048 + HTN)), rs_g[:, :]),
                            ((slice(0, 128), slice(2176, 2176 + HTN)), rs_u[:, :]),
                            ((slice(0, 1), slice(4096, 5120)), murow[0:1, :]),
                            ((slice(0, 1), slice(5120, 6144)), nrsrow[0:1, :]),
                            ((slice(0, 1), slice(6144, 7168)), s1row[0:1, :]),
                            ((slice(0, 1), slice(7168, 8192)), s2row[0:1, :])):
                        ev("sync", lambda e, dst_sl=dst_sl, src_ap=src_ap: e.dma_start(
                            out=dbg2_d.ap()[dst_sl[0], dst_sl[1]], in_=src_ap).then_inc(sem["SO0"], 16))
                        kb.bump("SO0", 16)
                for name, dd in dbg.items():
                    for j in range(DPT):
                        wait("vector", "SO0", kb.cnt["SO0"])
                        def df(e, j=j):
                            e.tensor_copy(xout[0][:, 0:1024], Xt[:, j * NT:(j + 1) * NT]).then_inc(sem["SV"], 1)
                        ev("vector", df)
                        dm = kb.bump("SV")
                        wait("sync", "SV", dm)
                        ev("sync", lambda e, dd=dd, j=j: e.dma_start(
                            out=dd.ap()[:, j * NT:(j + 1) * NT], in_=xout[0][:, 0:1024]).then_inc(sem["SO0"], 16))
                        kb.bump("SO0", 16)
                wait("sync", "SO0", kb.cnt["SO0"])
                wait("sync", "SO1", kb.cnt["SO1"])

        cur = x_ready
        if "ffn1" in sub_layers:
            cur = emit_ffn(1, cur)
        if "attn" in sub_layers:
            cur = emit_attn(cur)
        if "ffn2" in sub_layers:
            cur = emit_ffn(3, cur)
        if "ffn2f" in sub_layers:
            cur = emit_ffn(3, cur, force_first=True)
        emit_output(cur)

        @block.sync
        def _(e):
            for f in kb.ops["sync"]:
                f(e)

        @block.gpsimd
        def _(e):
            for f in kb.ops["gpsimd"]:
                f(e)

        @block.vector
        def _(e):
            for f in kb.ops["vector"]:
                f(e)

        @block.scalar
        def _(e):
            for f in kb.ops["scalar"]:
                f(e)

        @block.tensor
        def _(e):
            for f in kb.ops["tensor"]:
                f(e)

    return nc


_NC_CACHE = {}


def kernel(L0_attn_x, L0_attn_Wq, L0_attn_bq, L0_attn_Wk, L0_attn_bk,
           L0_attn_Wv, L0_attn_bv, L0_ffn_w_gate, L0_ffn_w_up, L0_ffn_w_down,
           L0_ffn_scale_gate, L0_ffn_scale_up, L0_ffn_scale_down,
           _debug_dump=(), _sub_layers=("ffn1", "attn", "ffn2")):
    x = np.ascontiguousarray(np.asarray(L0_attn_x).reshape(8192, 4096))
    wq = np.ascontiguousarray(np.asarray(L0_attn_Wq))
    wk = np.ascontiguousarray(np.asarray(L0_attn_Wk))
    wv = np.ascontiguousarray(np.asarray(L0_attn_Wv))
    bq = np.ascontiguousarray(np.asarray(L0_attn_bq))
    bk = np.ascontiguousarray(np.asarray(L0_attn_bk))
    bv = np.ascontiguousarray(np.asarray(L0_attn_bv))
    wg = np.ascontiguousarray(np.asarray(L0_ffn_w_gate))
    wu = np.ascontiguousarray(np.asarray(L0_ffn_w_up))
    wd = np.ascontiguousarray(np.asarray(L0_ffn_w_down))
    sg = np.asarray(L0_ffn_scale_gate, dtype=np.float32).reshape(1, 1)
    su = np.asarray(L0_ffn_scale_up, dtype=np.float32).reshape(1, 1)
    sd = np.asarray(L0_ffn_scale_down, dtype=np.float32).reshape(1, 1)
    idm = np.eye(128, dtype=ml_dtypes.bfloat16)

    key = (tuple(_debug_dump), tuple(_sub_layers))
    if key not in _NC_CACHE:
        _NC_CACHE[key] = build_nc(debug_dump=_debug_dump, sub_layers=_sub_layers)
    nc = _NC_CACHE[key]

    in_maps = []
    for c in range(8):
        in_maps.append({
            "x": x[c * 1024:(c + 1) * 1024],
            "wq": wq, "bq": bq, "wk": wk, "bk": bk, "wv": wv, "bv": bv,
            "wg": wg, "wu": wu, "wd": wd,
            "sg": sg, "su": su, "sd": sd, "idm": idm,
        })
    res = run_bass_kernel_spmd(nc, in_maps, core_ids=list(range(8)))
    out = np.concatenate([r["out"] for r in res.results], axis=0)
    if _debug_dump:
        return out.reshape(4, 2048, 4096), [
            {k: v for k, v in r.items() if k.startswith("dbg_")} for r in res.results]
    return out.reshape(4, 2048, 4096)
